# version 08
# baseline (speedup 1.0000x reference)
import contextlib
import numpy as np
import ml_dtypes
import concourse.bass as bass
import concourse.mybir as mybir
from concourse.bass_isa import ReduceOp
from concourse.library_config import attn as _LIB_ATTN
from concourse.library_overlay import lower_extended_insts
from concourse.bass_utils import run_bass_kernel_spmd

F32 = mybir.dt.float32
BF16 = mybir.dt.bfloat16
AF = mybir.ActivationFunctionType
OP = mybir.AluOpType
AX = mybir.AxisListType
RC = 12582912.0
S, D, FF, H, HD, CR, M, RMOD, NWV = 2048, 1024, 4096, 16, 64, 8, 16, 8, 8
NDC = D // 128
NFO = FF // 128
MIXSC = 0.15 / float(np.sqrt(8.0))
STEP_A = 6.0
NEG = -1.0e9

ENGINES = ("tensor", "vector", "scalar", "gpsimd", "sync")


class Prog:

    NPOOL = 8

    def __init__(self):
        self.steps = []
        self.floor = []
        self.dma_count = {e: 0 for e in ENGINES}

    def add(self, eng, emit, deps=(), is_dma=False, tag="", relaxed=()):
        assert eng in ENGINES, eng
        deps = [d for d in deps if d is not None] + self.floor
        rlx = frozenset(d for d in relaxed if d is not None)
        slot = None
        if is_dma:
            idx = self.dma_count[eng]
            self.dma_count[eng] += 1
            slot = (idx % self.NPOOL, idx // self.NPOOL + 1)
        self.steps.append([eng, emit, deps, is_dma, tag, False, 0, slot, rlx])
        return len(self.steps) - 1

    def dma(self, eng, out, in_, deps=(), tag="", **kw):
        return self.add(eng, (lambda e, o=out, i=in_, kw=kw: e.dma_start(out=o, in_=i, **kw)),
                        deps, is_dma=True, tag=tag)

    def barrier(self):
        last = {}
        for sid, st in enumerate(self.steps):
            last[st[0]] = sid
        self.floor = list(last.values())

    def link(self):
        steps = self.steps
        for st in steps:
            if st[3]:
                st[5] = True
            for d in st[2]:
                if d in st[8] and steps[d][0] == st[0] and not steps[d][3]:
                    continue
                steps[d][5] = True
        cum = {e: 0 for e in ENGINES}
        for st in steps:
            if st[5] and not st[3]:
                cum[st[0]] += 1
            st[6] = cum[st[0]]

    def sem_names(self):
        out = []
        for e in self.engines():
            out.append(("c", e))
            if self.dma_count[e]:
                for i in range(min(self.NPOOL, self.dma_count[e])):
                    out.append(("d", e, i))
        return out

    def emit_engine(self, eng_name, eng, sems):
        steps = self.steps
        wm = {}

        def wait(key, v):
            if wm.get(key, 0) < v:
                eng.wait_ge(sems[key], v)
                wm[key] = v

        for sid, st in enumerate(steps):
            if st[0] != eng_name:
                continue
            for d in st[2]:
                ds = steps[d]
                if ds[3]:
                    pi, use = ds[7]
                    wait(("d", ds[0], pi), 16 * use)
                elif not (d in st[8] and ds[0] == eng_name):
                    wait(("c", ds[0]), ds[6])
            if st[3]:
                pi, use = st[7]
                if use > 1:
                    wait(("d", eng_name, pi), 16 * (use - 1))
            inst = st[1](eng)
            if st[5]:
                if inst is None:
                    raise RuntimeError(f"step {sid} ({st[4]}) needs inc, no instruction")
                if st[3]:
                    pi, use = st[7]
                    inst.then_inc(sems[("d", eng_name, pi)], 16)
                else:
                    inst.then_inc(sems[("c", eng_name)], 1)

    def engines(self):
        return sorted({st[0] for st in self.steps}, key=ENGINES.index)


DEBUG = False
STAGE = "full"


def _build(nc):
    p = Prog()
    ctx = []
    stack = contextlib.ExitStack()

    def din(name, shape, dt):
        return nc.dram_tensor(name, list(shape), dt, kind="ExternalInput").ap()

    def dscr(name, shape, dt):
        return nc.dram_tensor(name, list(shape), dt, kind="Internal").ap()

    def sb(name, shape, dt):
        h = nc.sbuf_tensor(name, list(shape), dt)
        ctx.append(h)
        return h.__enter__()

    x_in = din("x", [S, D], F32)
    y_out = nc.dram_tensor("y", [S, D], F32, kind="ExternalOutput").ap()
    idf = din("idf", [128, 128], F32)
    idb = din("idb", [128, 128], BF16)
    w_bw1 = din("bw1", [FF, D], F32)
    w_bw2 = din("bw2", [D, FF], F32)
    w_conv_w = din("conv_w", [FF, 1, 3], F32)
    w_conv_b = din("conv_b", [FF], F32)
    kramp = din("kramp", [1, S], F32)
    gsel = din("gsel", [128, 8], F32)
    w_vw = din("vw", [D, D], F32)
    w_ow = din("ow", [D, D], F32)
    w_cqw = din("cqw", [D, H * CR], F32)
    w_ckw = din("ckw", [D, H * CR], F32)
    w_qmw = din("qmw", [D, H], F32)
    w_kmw = din("kmw", [D, H], F32)
    w_gw = din("gw", [D, H * M], F32)
    w_mw = din("mw", [D, H * RMOD], F32)
    w_basis = din("basis", [RMOD, H, M], F32)
    w_freqs = din("freqs", [H, M, NWV, 2], F32)
    w_amps = din("amps", [H, M, NWV], F32)
    w_phases = din("phases", [H, M, NWV], F32)
    w_gb = din("gb", [H * M], F32)
    w_mb = din("mb", [H * RMOD], F32)
    w_vb = din("vb", [D], F32)
    w_ob = din("ob", [D], F32)
    w_fw1 = din("fw1", [D, FF], F32)
    w_fb1 = din("fb1", [FF], F32)
    w_fw2 = din("fw2", [FF, D], F32)
    w_fb2 = din("fb2", [D], F32)
    w_ln = din("lnv", [4, D], F32)

    PB = []
    for i in range(8):
        h = nc.psum_tensor(f"pb{i}", [128, 512], F32)
        ctx.append(h)
        PB.append(h.__enter__())
    blr = [None] * 8

    XA = sb("XA", [128, NDC, S], F32)
    ARW = sb("ARW", [128, 32768], BF16)
    yqt = sb("yqt", [128, NDC, S], BF16)
    ones_b = sb("ones_b", [128, 1], BF16)
    s_idf = sb("s_idf", [128, 128], F32)
    s_idb = sb("s_idb", [128, 128], BF16)

    wcol = sb("wcol", [128, 8], F32)
    cwsb = sb("cwsb", [128, NFO, 3], F32)
    cbsb = sb("cbsb", [128, NFO], F32)
    sc_f1 = sb("sc_f1", [128, S], F32)
    sc_f2 = sb("sc_f2", [128, S], F32)
    PHU = sb("PHU", [128, 6656], F32)
    repA = PHU[:, 0:S]
    repG = PHU[:, S:2 * S]
    f_rep = PHU[:, 0:S]
    h3buf = PHU[:, S:2 * S]
    ph_h3t = PHU[:, 0:4096].rearrange("p (a b) -> p a b", a=NFO)
    ph_h3q = PHU[:, 4096:6144].bitcast(BF16).rearrange("p (a b) -> p a b", a=NFO)
    ph_g2 = PHU[:, 6144:6272]
    ph_f2 = PHU[:, 6272:6400]
    S1row = PHU[0:1, 2048:4096]
    S2row = PHU[0:1, 4096:6144]

    _stg = yqt[:, 0:2, :].rearrange("p a b -> p (a b)").bitcast(F32)
    xtm = _stg[:, 0:D]
    xtm2 = _stg[:, D:2 * D]

    def yqv(dc):
        return yqt[:, dc, :]

    sc_bf = yqt[:, 0, :]
    sc_bf2 = yqt[:, 1, :]

    if DEBUG:
        d_probe = dscr("scr_probe", [64, 128, 512], F32)
        pr_stg = sb("pr_stg", [128, 512], F32)
        pr_state = [0, None]

        def probe(src_ap, deps):
            slot = pr_state[0]
            pr_state[0] += 1
            nr = src_ap.shape[0]
            ncol = src_ap.shape[-1]
            cp = p.add("vector", (lambda e: e.tensor_copy(pr_stg[0:nr, 0:ncol], src_ap)),
                       list(deps) + [pr_state[1]], tag=f"probe_cp{slot}")
            dm = p.dma("gpsimd", d_probe[slot][0:nr, 0:ncol], pr_stg[0:nr, 0:ncol],
                       deps=[cp], tag=f"probe_dma{slot}")
            pr_state[1] = dm
            return cp
    d_h3 = dscr("scr_h3", [FF, S], BF16)
    d_x1 = dscr("scr_x1", [D, S], F32)
    d_ctx = dscr("scr_ctx", [D, S], BF16)
    d_u = dscr("scr_u", [FF, S], BF16)
    d_stats = dscr("scr_stats", [8, S], F32)
    d_dext = dscr("scr_dext", [H, 2560], F32)
    d_w1t = dscr("scr_w1t", [NFO * NDC * 128, 128], BF16)
    d_w2t = dscr("scr_w2t", [NDC * NFO * 128, 128], BF16)

    ld_idf = p.dma("sync", s_idf[:], idf, tag="ld_idf")
    ld_idb = p.dma("sync", s_idb[:], idb, tag="ld_idb")
    ones_dep = p.add("vector", lambda e: e.memset(ones_b[:], 1.0), tag="ones_b")
    rc_dep = p.add("vector", lambda e: e.memset(wcol[:, 3:4], RC), tag="rc_col")
    nrc_dep = p.add("vector", lambda e: e.memset(wcol[:, 0:1], -RC), tag="nrc_col")
    p.add("gpsimd", lambda e: e.load_library(_LIB_ATTN), tag="load_lib")

    def mm(bank, lhsT, rhs, start, stop, deps, tag="mm", ncols=None, nrows=128, rowoff=0, coloff=0):
        n = ncols if ncols is not None else rhs.shape[-1]
        dd = list(deps)
        if start and blr[bank] is not None:
            dd.append(blr[bank])
        return p.add("tensor", (lambda e, l=lhsT, r=rhs, s=start, t=stop, b=PB[bank], n=n,
                                nr=nrows, ro=rowoff, co=coloff:
                                e.matmul(b[ro:ro + nr, co:co + n], l, r, start=s, stop=t)), dd, tag=tag)

    def consume(bank, sid):
        blr[bank] = sid

    prev = [None, None]
    for tt in range(16):
        buf = xtm if tt % 2 == 0 else xtm2
        ld = p.dma("sync", buf[:], x_in[tt * 128:(tt + 1) * 128, :],
                   deps=[prev[tt % 2]], tag=f"ldx{tt}")
        for dc in range(NDC):
            bi = dc % 2
            tr = p.add("tensor", (lambda e, b=buf, dc=dc, bi=bi:
                                  e.transpose(PB[bi][:, 0:128], b[:, dc * 128:(dc + 1) * 128], s_idf[:])),
                       [ld, ld_idf, blr[bi]], tag=f"trx{tt}_{dc}")
            cp = p.add("vector", (lambda e, dc=dc, tt=tt, bi=bi:
                                  e.tensor_copy(XA[:, dc, tt * 128:(tt + 1) * 128], PB[bi][:, 0:128])),
                       [tr], tag=f"cpx{tt}_{dc}")
            consume(bi, cp)
            prev[tt % 2] = cp
    x_dep = max(prev)

    ld_cw = p.dma("sync", cwsb[:], bass.AP(w_conv_w.tensor, 0, [[3, 128], [384, NFO], [1, 3]]), tag="ld_cw", allow_slow_non_contiguous=True)
    ld_cb = p.dma("sync", cbsb[:], bass.AP(w_conv_b.tensor, 0, [[1, 128], [128, NFO]]), tag="ld_cb", allow_slow_non_contiguous=True)

    wld = sc_f2[:, 0:FF // 2]
    wtern = PHU[:, S:2 * S].bitcast(BF16)[:, 0:FF // 2]
    wstg = PHU[:, S:2 * S].bitcast(BF16)[:, FF // 2:FF]

    def record_accums(specs):
        states = []
        seq = []
        for si, (tag, w_dram, nrt, sum_col, rd_col, bufs, queues) in enumerate(specs):
            zz = p.add("vector", (lambda e, sum_col=sum_col: e.memset(
                wcol[:, sum_col:sum_col + 1], 0.0)), tag=f"{tag}_z")
            states.append({"last": zz, "prev": [None] * len(bufs), "k": 0})
            seq.append([(si, i, hf) for i in range(nrt) for hf in (0, 1)])
        idx = [0] * len(seq)
        total = sum(len(s) for s in seq)
        for j in range(total):
            si = max(range(len(seq)),
                     key=lambda x: (len(seq[x]) - idx[x]) / len(seq[x]) if idx[x] < len(seq[x]) else -1)
            tag, w_dram, nrt, sum_col, rd_col, bufs, queues = specs[si]
            st = states[si]
            _, i, hf = seq[si][idx[si]]
            idx[si] += 1
            HL = w_dram.shape[1] // 2
            b = st["k"] % len(bufs)
            buf = bufs[b]
            ld = p.dma(queues[st["k"] % len(queues)], buf[:, 0:HL],
                       w_dram[i * 128:(i + 1) * 128, hf * HL:(hf + 1) * HL],
                       deps=[st["prev"][b]], tag=f"{tag}_l{i}_{hf}")
            rd = p.add("vector", (lambda e, buf=buf, rd_col=rd_col, HL=HL: e.tensor_reduce(
                wcol[:, rd_col:rd_col + 1], buf[:, 0:HL], AX.X,
                OP.add, apply_absolute_value=True)),
                [ld], tag=f"{tag}_r{i}_{hf}")
            st["prev"][b] = rd
            st["last"] = p.add("vector", (lambda e, sum_col=sum_col, rd_col=rd_col: e.tensor_add(
                wcol[:, sum_col:sum_col + 1], wcol[:, sum_col:sum_col + 1],
                wcol[:, rd_col:rd_col + 1])),
                [rd, st["last"]], tag=f"{tag}_a{i}_{hf}")
            st["k"] += 1
        return [st["last"] for st in states]

    def prep_bitweight(tag, w_dram, nrt, mw_slot, dram_stash, to_arena,
                       sum_col, acc_last, tern_queue, tern_bufs, tern_gate=None):
        IN = w_dram.shape[1]
        nic = IN // 128
        HL = IN // 2
        nih = nic // 2
        ar = p.add("gpsimd", (lambda e, sum_col=sum_col: e.partition_all_reduce(
            wcol[:, sum_col:sum_col + 1], wcol[:, sum_col:sum_col + 1], 128, ReduceOp.add)),
            [acc_last], tag=f"{tag}_ar")
        nel = float(nrt * 128 * IN)
        mw = p.add("vector", (lambda e, nel=nel, mw_slot=mw_slot, sum_col=sum_col: e.tensor_scalar(
            wcol[:, mw_slot:mw_slot + 1], wcol[:, sum_col:sum_col + 1], 1.0 / nel, 1e-5,
            OP.mult, OP.max)),
            [ar], tag=f"{tag}_mw")
        sr = p.add("vector", (lambda e, mw_slot=mw_slot, sum_col=sum_col: e.reciprocal(
            wcol[:, sum_col:sum_col + 1], wcol[:, mw_slot:mw_slot + 1])), [mw], tag=f"{tag}_sr")
        if DEBUG and tag == "w1p":
            probe(wcol[:, 0:8], [mw])
            probe(wcol[:, 0:8], [sr])
        prev_t1 = [tern_gate] * len(tern_bufs)
        prev_t3 = None
        prev_st = None
        last_tile = None
        kk = 0
        for i in range(nrt):
            for hf in range(2):
                tb = kk % len(tern_bufs)
                buf = tern_bufs[tb]
                kk += 1
                ld = p.dma(tern_queue, buf[:, 0:HL], w_dram[i * 128:(i + 1) * 128, hf * HL:(hf + 1) * HL],
                           deps=[sr, prev_t1[tb]], tag=f"{tag}_m{i}_{hf}")
                t1 = p.add("scalar", (lambda e, sum_col=sum_col, buf=buf: e.activation(
                    sc_f1[:, 0:HL], buf[:, 0:HL], AF.Identity,
                    scale=wcol[:, sum_col:sum_col + 1], bias=wcol[:, 3:4])),
                    [ld, rc_dep, sr, prev_t3], tag=f"{tag}_t1_{i}_{hf}")
                prev_t1[tb] = t1
                if DEBUG and tag == "w1p" and i == 0 and hf == 0:
                    prev_t1[tb] = probe(buf[:, 0:512], [ld, t1])
                    probe(sc_f1[:, 0:512], [t1])
                t2 = p.add("vector", (lambda e: e.tensor_scalar(sc_f1[:, 0:HL], sc_f1[:, 0:HL],
                                                                RC, 1.0, OP.subtract, OP.min)),
                           [t1], tag=f"{tag}_t2_{i}_{hf}")
                t3 = p.add("vector", (lambda e: e.tensor_scalar(wtern[:, 0:HL], sc_f1[:, 0:HL],
                                                                -1.0, None, OP.max)),
                           [t2], tag=f"{tag}_t3_{i}_{hf}")
                prev_t3 = t3
                if DEBUG and tag == "w1p" and i == 0 and hf == 0:
                    probe(sc_f1[:, 0:512], [t2])
                    probe(wtern.bitcast(F32)[:, 0:256], [t3])
                evs = []
                for j in range(nih):
                    ic = hf * nih + j
                    bi = 2 + j % 2
                    tr = p.add("tensor", (lambda e, j=j, bi=bi:
                                          e.transpose(PB[bi].bitcast(BF16)[:, 0:128],
                                                      wtern[:, j * 128:(j + 1) * 128], s_idb[:])),
                               [t3, ld_idb, blr[bi]], tag=f"{tag}_tr{i}_{ic}")
                    if to_arena:
                        cp = p.add("vector", (lambda e, k=i * nic + ic, bi=bi:
                                              e.tensor_copy(ARW[:, k * 128:(k + 1) * 128],
                                                            PB[bi].bitcast(BF16)[:, 0:128])),
                                   [tr], tag=f"{tag}_cp{i}_{ic}")
                    else:
                        cp = p.add("vector", (lambda e, j=j, bi=bi:
                                              e.tensor_copy(wstg[:, j * 128:(j + 1) * 128],
                                                            PB[bi].bitcast(BF16)[:, 0:128])),
                                   [tr, prev_st], tag=f"{tag}_cp{i}_{ic}")
                    consume(bi, cp)
                    evs.append(cp)
                src_ap = ARW[:, (i * nic + hf * nih) * 128:(i * nic + (hf + 1) * nih) * 128]                     if to_arena else wstg[:, 0:HL]
                dst_ap = bass.AP(dram_stash.tensor, (i * nic + hf * nih) * 128 * 128,
                                 [[128, 128], [128 * 128, nih], [1, 128]])
                st = p.dma("gpsimd", dst_ap, src_ap, deps=evs, tag=f"{tag}_st{i}_{hf}")
                prev_st = st
                last_tile = st
        return last_tile, mw

    w2acc = [yqt[:, 4:6, :].rearrange("p a b -> p (a b)").bitcast(F32),
             yqt[:, 6:8, :].rearrange("p a b -> p (a b)").bitcast(F32)]
    acc1, acc2 = record_accums([
        ("w1p", w_bw1, NFO, 1, 2, [sc_f2[:, 0:512], sc_f2[:, 512:1024]],
         ("scalar",)),
        ("w2p", w_bw2, NDC, 6, 7, w2acc, ("gpsimd",)),
    ])
    w1_rdy, w1_mw = prep_bitweight("w1p", w_bw1, NFO, 4, d_w1t, to_arena=True,
                                   sum_col=1, acc_last=acc1, tern_queue="scalar",
                                   tern_bufs=[sc_f2[:, 0:512], sc_f2[:, 512:1024]])
    w2_rdy, w2_mw = prep_bitweight("w2p", w_bw2, NDC, 5, d_w2t, to_arena=False,
                                   sum_col=6, acc_last=acc2, tern_queue="gpsimd",
                                   tern_bufs=[sc_f2[:, 0:2048], w2acc[0]],
                                   tern_gate=w1_rdy)

    def emit_ln_stats(tag, xv, deps, eps=1e-5, phu_guard=None):
        mm_last = []
        prev_c = [None, None]
        for dc in range(NDC):
            c1 = p.add("vector", (lambda e, dc=dc: e.tensor_copy(sc_bf, xv(dc))),
                       deps + [prev_c[0]], tag=f"{tag}_cb{dc}")
            c2 = p.add("scalar", (lambda e, dc=dc: e.activation(sc_bf2, xv(dc), AF.Square)),
                       deps + [prev_c[1]], tag=f"{tag}_sq{dc}")
            s_last = []
            for tt in range(4):
                b0, b1 = tt, 4 + tt
                m1 = mm(b0, ones_b[:], sc_bf[:, tt * 512:(tt + 1) * 512],
                        start=(dc == 0), stop=(dc == NDC - 1), deps=[c1, ones_dep],
                        tag=f"{tag}_s1_{dc}_{tt}", nrows=1)
                m2 = mm(b1, ones_b[:], sc_bf2[:, tt * 512:(tt + 1) * 512],
                        start=(dc == 0), stop=(dc == NDC - 1), deps=[c2, ones_dep],
                        tag=f"{tag}_s2_{dc}_{tt}", nrows=1)
                s_last.append((tt, m1, m2))
            prev_c = [s_last[-1][1], s_last[-1][2]]
            if dc == NDC - 1:
                mm_last = s_last
        ev1, ev2 = [], []
        for tt, m1, m2 in mm_last:
            b0, b1 = tt, 4 + tt
            e1 = p.add("vector", (lambda e, tt=tt, b0=b0: e.tensor_copy(
                S1row[:, tt * 512:(tt + 1) * 512], PB[b0][0:1, 0:512])), [m1, phu_guard],
                tag=f"{tag}_e1{tt}")
            consume(b0, e1)
            e2 = p.add("vector", (lambda e, tt=tt, b1=b1: e.tensor_copy(
                S2row[:, tt * 512:(tt + 1) * 512], PB[b1][0:1, 0:512])), [m2], tag=f"{tag}_e2{tt}")
            consume(b1, e2)
            ev1.append(e1)
            ev2.append(e2)
        p1 = p.add("gpsimd", lambda e: e.partition_broadcast(repA[:, :], S1row, 128),
                   ev1, tag=f"{tag}_pb1")
        p2 = p.add("gpsimd", lambda e: e.partition_broadcast(repG[:, :], S2row, 128),
                   ev2 + [p1], tag=f"{tag}_pb2")
        a1 = p.add("vector", (lambda e: e.tensor_scalar(repA[:, :], repA[:, :], 1.0 / D, None, OP.mult)),
                   [p1], tag=f"{tag}_mu")
        a2 = p.add("vector", lambda e: e.tensor_mul(sc_f1[:], repA[:, :], repA[:, :]),
                   [a1], tag=f"{tag}_mu2")
        a3 = p.add("vector", (lambda e: e.scalar_tensor_tensor(repG[:, :], repG[:, :], 1.0 / D,
                                                               sc_f1[:], OP.mult, OP.subtract)),
                   [a2, p2], tag=f"{tag}_var")
        a4 = p.add("vector", (lambda e, eps=eps: e.tensor_scalar(repG[:, :], repG[:, :], eps, None, OP.add)),
                   [a3], tag=f"{tag}_veps")
        a5 = p.add("vector", lambda e: e.reciprocal(repG[:, :], repG[:, :]), [a4], tag=f"{tag}_rcp")
        a6 = p.add("scalar", lambda e: e.activation(repG[:, :], repG[:, :], AF.Sqrt), [a5], tag=f"{tag}_rst")
        return a1, a6

    def emit_quant(tag, xv, deps, mu_dep, inv_dep, mw_slot, extra_f, guard=None):
        mnbuf = sc_f1[:, 0:S]
        za = p.add("vector", lambda e: e.memset(sc_f2[:], -3.0e38), deps + [guard], tag=f"{tag}_z")
        zb = p.add("vector", lambda e: e.memset(mnbuf, 3.0e38), deps + [guard], tag=f"{tag}_z2")
        last = zb
        for dc in range(NDC):
            t1 = p.add("vector", (lambda e, dc=dc: e.tensor_tensor(sc_f2[:], sc_f2[:], xv(dc), OP.max)),
                       deps + [za, last], tag=f"{tag}_mx{dc}")
            t2 = p.add("vector", (lambda e, dc=dc: e.tensor_tensor(mnbuf, mnbuf, xv(dc), OP.min)),
                       [t1], tag=f"{tag}_mn{dc}")
            last = t2
        ng = p.add("vector", (lambda e: e.tensor_scalar(mnbuf, mnbuf, -1.0, None, OP.mult)),
                   [last], tag=f"{tag}_ng")
        ar1 = p.add("gpsimd", lambda e: e.partition_all_reduce(sc_f2[:], sc_f2[:], 128, ReduceOp.max),
                    [last], tag=f"{tag}_ar1")
        ar2 = p.add("gpsimd", lambda e: e.partition_all_reduce(mnbuf, mnbuf, 128, ReduceOp.max),
                    [ng, ar1], tag=f"{tag}_ar2")
        c1 = p.add("vector", (lambda e: e.tensor_sub(sc_f2[:], sc_f2[:], repA[:, :])),
                   [ar1, mu_dep], tag=f"{tag}_c1")
        c2 = p.add("vector", (lambda e: e.tensor_add(sc_f1[:], mnbuf, repA[:, :])),
                   [ar2], tag=f"{tag}_c2")
        ar = p.add("vector", (lambda e: e.tensor_tensor(sc_f2[:], sc_f2[:], sc_f1[:], OP.max)),
                   [c1, c2], tag=f"{tag}_m")
        r1 = p.add("vector", (lambda e: e.tensor_tensor(sc_f2[:], sc_f2[:], repG[:, :], OP.mult)),
                   [ar, inv_dep], tag=f"{tag}_my")
        r2 = p.add("vector", (lambda e: e.tensor_scalar(sc_f2[:], sc_f2[:], 1e-5, None, OP.max)),
                   [r1], tag=f"{tag}_cl")
        r3 = p.add("vector", lambda e: e.reciprocal(sc_f1[:], sc_f2[:]), [r2], tag=f"{tag}_rc")
        r4 = p.add("vector", (lambda e: e.tensor_scalar(sc_f1[:], sc_f1[:], 127.0, None, OP.mult)),
                   [r3], tag=f"{tag}_127")
        r5 = p.add("vector", lambda e: e.tensor_tensor(repG[:, :], repG[:, :], sc_f1[:], OP.mult),
                   [r4], tag=f"{tag}_g")
        r6 = p.add("vector", lambda e: e.tensor_tensor(repA[:, :], repA[:, :], repG[:, :], OP.mult),
                   [r5], tag=f"{tag}_mg")
        qd = []
        prev_q3 = None
        for dc in range(NDC):
            q1 = p.add("vector", (lambda e, dc=dc: e.tensor_mul(sc_f1[:], xv(dc), repG[:, :])),
                       deps + [r6, prev_q3], tag=f"{tag}_q1{dc}")
            q2 = p.add("vector", (lambda e: e.tensor_sub(sc_f1[:], sc_f1[:], repA[:, :])),
                       [q1], tag=f"{tag}_q2{dc}")
            q3a = p.add("scalar", (lambda e: e.activation(sc_f1[:], sc_f1[:], AF.Identity,
                                                          bias=wcol[:, 3:4])),
                        [q2, rc_dep], tag=f"{tag}_q3a{dc}")
            q3 = p.add("scalar", (lambda e, dc=dc: e.activation(yqv(dc), sc_f1[:], AF.Identity,
                                                                bias=wcol[:, 0:1])),
                       [q3a, nrc_dep], tag=f"{tag}_q3{dc}")
            prev_q3 = q3
            qd.append(q3)
        rf = p.add("vector", (lambda e, mw_slot=mw_slot, extra_f=extra_f: e.tensor_scalar(
            f_rep[:, :], sc_f2[:], wcol[:, mw_slot:mw_slot + 1], extra_f / 127.0,
            OP.mult, OP.mult)), [r2] + qd, tag=f"{tag}_f")
        return qd, [rf]

    def emit_bitffn_sweep1(tag, w1_ready, qd, frep):
        sts = []
        prev_sil = None
        prev_st = None
        prev_b1 = None
        for fo in range(NFO):
            grp = (fo % 2) * 4
            if DEBUG and tag == "A" and fo in (0, 31):
                probe(ARW.bitcast(F32)[:, 15872:16384], [prev_st, w1_ready] + qd)
            mml = []
            for tt in range(4):
                bi = grp + tt
                sid = None
                for ic in range(NDC):
                    sid = mm(bi, ARW[:, (fo * NDC + ic) * 128:(fo * NDC + ic) * 128 + 128],
                             yqt[:, ic, tt * 512:(tt + 1) * 512],
                             start=(ic == 0), stop=(ic == NDC - 1),
                             deps=(qd + [w1_ready] if ic == 0 else []), tag=f"{tag}1mm{fo}_{tt}_{ic}")
                mml.append(sid)
            evs = []
            if DEBUG and tag == "A" and fo == 31:
                probe(PB[grp][:, 0:512], [mml[0]])
                probe(f_rep[:, 0:512], frep + [mml[0]])
            for tt in range(4):
                bi = grp + tt
                sid = p.add("vector", (lambda e, tt=tt, bi=bi: e.tensor_mul(
                    h3buf[:, tt * 512:(tt + 1) * 512], PB[bi][:, 0:512], f_rep[:, tt * 512:(tt + 1) * 512])),
                    [mml[tt], prev_st] + frep, tag=f"{tag}1ev{fo}_{tt}")
                consume(bi, sid)
                evs.append(sid)
            if DEBUG and tag == "A" and fo in (0, 31):
                probe(h3buf[:, 0:512], evs)
            a1 = p.add("vector", (lambda e, fo=fo: e.tensor_scalar(
                sc_f1[:], h3buf[:], cwsb[:, fo, 1:2], cbsb[:, fo:fo + 1], OP.mult, OP.add)),
                evs + [ld_cw, ld_cb, prev_sil, prev_b1], tag=f"{tag}cva{fo}")
            b1 = p.add("vector", (lambda e, fo=fo: e.scalar_tensor_tensor(
                sc_f1[:, 1:S], h3buf[:, 0:S - 1], cwsb[:, fo, 0:1], sc_f1[:, 1:S], OP.mult, OP.add)),
                [a1], tag=f"{tag}cvb{fo}")
            prev_b1 = b1
            c1 = p.add("vector", (lambda e, fo=fo: e.scalar_tensor_tensor(
                sc_f1[:, 0:S - 1], h3buf[:, 1:S], cwsb[:, fo, 2:3], sc_f1[:, 0:S - 1], OP.mult, OP.add)),
                [b1], tag=f"{tag}cvc{fo}")
            if DEBUG and tag == "A" and fo in (0, 31):
                probe(sc_f1[:, 0:512], [c1])
            s1 = p.add("scalar", (lambda e: e.activation(h3buf[:], sc_f1[:], AF.Silu)),
                       [c1], tag=f"{tag}si{fo}")
            prev_sil = s1
            if DEBUG and tag == "A" and fo in (0, 31):
                probe(h3buf[:, 0:512], [s1])
            st = p.dma("gpsimd", d_h3[fo * 128:(fo + 1) * 128, :], h3buf[:], deps=[s1], tag=f"{tag}1st{fo}")
            prev_st = st
            sts.append(st)
        return sts

    def load_arena_from_stash(tag, stash, ntiles, deps):
        out = []
        per = ntiles // 8
        for i in range(8):
            src = bass.AP(stash.tensor, i * per * 128 * 128,
                          [[128, 128], [128 * 128, per], [1, 128]])
            out.append(p.dma("sync", ARW[:, i * per * 128:(i + 1) * per * 128], src,
                             deps=deps, tag=f"{tag}_{i}"))
        return out

    def emit_bitffn_sweep2(tag, w2_ready, mw2_dep, mw2_slot):
        upd = []
        prev_mmlast = None
        prev_upd = [None] * 8
        T = 512
        h3t = yqt[:, :, :].rearrange("p a b -> p (a b)").rearrange(
            "p (a b) -> p a b", a=NFO)
        g2rep = PHU[:, 0:512]
        f2rep = PHU[:, 512:1024]
        for ti in range(4):
            t0 = ti * T
            sap = bass.AP(d_h3.tensor, t0, [[S, 128], [128 * S, NFO], [1, T]])
            ld = p.dma("sync", h3t[:, :, :], sap, deps=[prev_mmlast], tag=f"{tag}ld{ti}")
            rd = p.add("vector", (lambda e: e.tensor_reduce(
                sc_f2[:, 0:T], h3t[:, :, :].transpose([0, 2, 1]), AX.X, OP.max,
                apply_absolute_value=True)), [ld], tag=f"{tag}rd{ti}")
            ar = p.add("gpsimd", (lambda e: e.partition_all_reduce(sc_f2[:, 0:T], sc_f2[:, 0:T],
                                                                   128, ReduceOp.max)),
                       [rd], tag=f"{tag}ar{ti}")
            c1 = p.add("vector", (lambda e: e.tensor_scalar(sc_f2[:, 0:T], sc_f2[:, 0:T],
                                                            1e-5, None, OP.max)), [ar], tag=f"{tag}cl{ti}")
            f2 = p.add("vector", (lambda e, mw2_slot=mw2_slot: e.tensor_scalar(
                f2rep[:], sc_f2[:, 0:T], wcol[:, mw2_slot:mw2_slot + 1], 0.5 / 127.0,
                OP.mult, OP.mult)), [c1, mw2_dep], tag=f"{tag}f{ti}")
            g1 = p.add("vector", lambda e: e.reciprocal(g2rep[:], sc_f2[:, 0:T]), [c1], tag=f"{tag}rc{ti}")
            g2 = p.add("vector", (lambda e: e.tensor_scalar(g2rep[:], g2rep[:], 127.0, None, OP.mult)),
                       [g1], tag=f"{tag}g{ti}")
            g2b = g2rep.unsqueeze(1).broadcast_to((128, NFO, T))
            q1 = p.add("vector", (lambda e, g2b=g2b: e.tensor_tensor(
                h3t[:, :, :], h3t[:, :, :], g2b, OP.mult)), [g2, ld], tag=f"{tag}q1_{ti}")
            q3 = p.add("vector", (lambda e: e.tensor_scalar(
                h3t[:, :, :], h3t[:, :, :], RC, RC, OP.add, OP.subtract)),
                [q1], tag=f"{tag}q3_{ti}")
            for do in range(NDC):
                sid = None
                for kc in range(NFO):
                    sid = mm(do, ARW[:, (do * NFO + kc) * 128:(do * NFO + kc) * 128 + 128],
                             h3t[:, kc, :], start=(kc == 0), stop=(kc == NFO - 1),
                             deps=([q3] + w2_ready if kc == 0 else []), ncols=T,
                             tag=f"{tag}2mm{ti}_{do}_{kc}")
                if do == NDC - 1:
                    prev_mmlast = sid
                t1 = p.add("vector", (lambda e, do=do: e.tensor_mul(sc_f1[:, 0:T], PB[do][:, 0:T],
                                                                    f2rep[:])),
                           [sid, f2, prev_upd[do]], tag=f"{tag}ev{ti}_{do}")
                consume(do, t1)
                u1 = p.add("vector", (lambda e, do=do, t0=t0: e.tensor_add(
                    XA[:, do, t0:t0 + T], XA[:, do, t0:t0 + T], sc_f1[:, 0:T])),
                    [t1], tag=f"{tag}up{ti}_{do}")
                prev_upd[do] = u1
                upd.append(u1)
        return upd

    def emit_bitffn(tag, w1_ready, w2_ready_stash, mw2_dep, x_deps):
        muD, invD = emit_ln_stats(tag + "ln", lambda dc: XA[:, dc, :], x_deps,
                                  phu_guard=w2_rdy)
        qd, frep = emit_quant(tag + "q", lambda dc: XA[:, dc, :], x_deps, muD, invD, 4, 1.0, guard=w2_rdy)
        if DEBUG and tag == "A":
            probe(yqv(0)[:, 0:512], qd)
            probe(f_rep[:, 0:512], frep)
            probe(repG[:, 0:512], qd)
        sts = emit_bitffn_sweep1(tag, w1_ready, qd, frep)
        p.barrier()
        w2ld = load_arena_from_stash(tag + "w2", d_w2t, NDC * NFO, [w2_ready_stash])
        upd = emit_bitffn_sweep2(tag, w2ld, mw2_dep, 5)
        p.barrier()
        return upd


    NBLK = 6

    def hb(h):
        return (32 * (h % 3), h // 3)

    def emit_attention(x_deps):
        rhsQ = XA[:, 0:3, :].rearrange("p a b -> p (a b)").bitcast(BF16) \
                 .rearrange("p (a b) -> p a b", a=NBLK)
        lhsK = XA[:, 3:6, :].rearrange("p a b -> p (a b)").bitcast(BF16) \
                 .rearrange("p (a b) -> p a b", a=NBLK)
        vfm = ARW[:, 0:16384]
        Pt = ARW[:, 16384:24576]
        vw_a = ARW[:, 16384:24576]
        ow_a = ARW[:, 16384:24576]
        cq_a = ARW[:, 24576:25600]
        ck_a = ARW[:, 25600:26624]
        qmw_a = ARW[:, 26624:26752]
        kmw_a = ARW[:, 26752:26880]
        vaug = [ARW[:, 26880 + r * 1040:26880 + (r + 1) * 1040] for r in range(2)]
        ctxstg = ARW[0:64, 29056:29568]
        acol = PHU[:, 6400:6656]
        a_vb = acol[:, 0:8]
        a_ob = acol[:, 8:16]
        a_fb1 = acol[:, 16:48]
        a_fb2 = acol[:, 48:56]
        a_ln = acol[:, 56:88]
        a_gb = acol[:, 88:90]
        a_mb = acol[:, 90:91]
        a_pool = acol[:, 91:99]
        a_modc = acol[:, 99:100]
        a_fq = acol[:, 100:132]
        a_ph = acol[:, 132:148]
        a_am = acol[:, 148:164]
        a_wts = acol[:, 164:166]
        a_gsel = acol[:, 166:174]
        gw_a = PHU[:, 4096:6144]
        mw_a = PHU[:, 2048:3072]
        bas_a = PHU[0:8, 3072:3328]
        glsb = PHU[:, 3328:3330]
        G16 = PHU[0:16, 3332:3348]
        G16b = PHU[0:16, 3348:3356]
        modT = PHU[0:8, 3364:3380]
        GB = [PHU[:, 0:2432], PHU[:, 2432:4864]]
        rrow = PHU[0:1, 4864:5376]
        rrep = PHU[0:64, 5376:5888]

        muZ, invZ = emit_ln_stats("zln", lambda dc: XA[:, dc, :], x_deps)
        sv1 = p.dma("gpsimd", d_stats[0:1, :], repA[0:1, :], deps=[muZ, invZ], tag="sv_mu")
        sv2 = p.dma("gpsimd", d_stats[1:2, :], repG[0:1, :], deps=[invZ], tag="sv_inv")
        zd = []
        for dc in range(NDC):
            z1 = p.add("vector", (lambda e, dc=dc: e.tensor_sub(sc_f1[:], XA[:, dc, :], repA[:, :])),
                       x_deps + [muZ, invZ, sv1], tag=f"z1_{dc}")
            z2 = p.add("vector", (lambda e, dc=dc: e.tensor_mul(yqv(dc), sc_f1[:], repG[:, :])),
                       [z1, sv2], tag=f"z2_{dc}")
            zd.append(z2)
        spl = [p.dma("gpsimd", d_x1[dc * 128:(dc + 1) * 128, :], XA[:, dc, :],
                     deps=[zd[-1]], tag=f"spl{dc}") for dc in range(NDC)]

        krrow = sc_f2[0:1, :]
        ld_kr = p.dma("sync", krrow, kramp, deps=x_deps, tag="ldkr")
        prow = sc_f1[0:1, :]
        r1 = p.add("vector", (lambda e: e.tensor_scalar(prow, krrow, 1.0 / S, None, OP.mult)),
                   [ld_kr], tag="p1row")
        sv3 = p.dma("gpsimd", d_stats[4:5, :], prow, deps=[r1], tag="sv_p1")
        r2 = p.add("vector", (lambda e: e.tensor_scalar(prow, krrow, 1e-6, None, OP.add)),
                   [sv3], tag="p2a")
        r3 = p.add("scalar", (lambda e: e.activation(prow, prow, AF.Sqrt)), [r2], tag="p2b")
        r4 = p.add("vector", (lambda e: e.tensor_scalar(prow, prow, float(1.0 / np.sqrt(S)), None, OP.mult)),
                   [r3], tag="p2c")
        r5 = p.add("vector", (lambda e: e.memset(prow[:, 0:1], 0.0)), [r4], tag="p2z")
        sv4 = p.dma("gpsimd", d_stats[5:6, :], prow, deps=[r5], tag="sv_p2")
        r6 = p.add("scalar", (lambda e: e.activation(prow, krrow, AF.Square)), [sv4], tag="w3a")
        r7 = p.add("scalar", (lambda e: e.activation(prow, prow, AF.Exp, scale=float(-1.0 / 524288.0))),
                   [r6], tag="w3b")
        r8 = p.add("vector", (lambda e: e.tensor_scalar(prow, prow, 3.0, None, OP.mult)), [r7], tag="w3c")
        sv5 = p.dma("gpsimd", d_stats[6:7, :], prow, deps=[r8], tag="sv_w3")

        ld_vw = [p.dma("gpsimd", vw_a[:, dc * 1024:(dc + 1) * 1024],
                       w_vw[dc * 128:(dc + 1) * 128, :], deps=x_deps, tag=f"ldvw{dc}") for dc in range(NDC)]
        ld_cq = [p.dma("gpsimd", cq_a[:, dc * 128:(dc + 1) * 128],
                       w_cqw[dc * 128:(dc + 1) * 128, :], deps=x_deps, tag=f"ldcq{dc}") for dc in range(NDC)]
        ld_ck = [p.dma("gpsimd", ck_a[:, dc * 128:(dc + 1) * 128],
                       w_ckw[dc * 128:(dc + 1) * 128, :], deps=x_deps, tag=f"ldck{dc}") for dc in range(NDC)]
        ld_qm = [p.dma("gpsimd", qmw_a[:, dc * 16:(dc + 1) * 16],
                       w_qmw[dc * 128:(dc + 1) * 128, :], deps=x_deps, tag=f"ldqm{dc}") for dc in range(NDC)]
        ld_km = [p.dma("gpsimd", kmw_a[:, dc * 16:(dc + 1) * 16],
                       w_kmw[dc * 128:(dc + 1) * 128, :], deps=x_deps, tag=f"ldkm{dc}") for dc in range(NDC)]
        def colload(name, dst, wap, n):
            return p.dma("sync", dst, bass.AP(wap.tensor, 0, [[1, 128], [128, n]]),
                         tag=name, allow_slow_non_contiguous=True)
        ld_vb = colload("ldvb", a_vb, w_vb, 8)
        ld_ob = colload("ldob", a_ob, w_ob, 8)
        ld_fb1 = colload("ldfb1", a_fb1, w_fb1, 32)
        ld_fb2 = colload("ldfb2", a_fb2, w_fb2, 8)
        ld_gb = colload("ldgb", a_gb, w_gb, 2)
        ld_mb = colload("ldmb", a_mb, w_mb, 1)
        ld_ln = p.dma("sync", a_ln, bass.AP(w_ln.tensor, 0, [[1, 128], [1024, 4], [128, 8]]),
                      tag="ldln", allow_slow_non_contiguous=True)
        ld_fq = p.dma("sync", a_fq, bass.AP(w_freqs.tensor, 0, [[16, 128], [2048, 2], [1, 16]]),
                      tag="ldfq", allow_slow_non_contiguous=True)
        ld_ph = p.dma("sync", a_ph, bass.AP(w_phases.tensor, 0, [[8, 128], [1024, 2], [1, 8]]),
                      tag="ldph", allow_slow_non_contiguous=True)
        ld_am = p.dma("sync", a_am, bass.AP(w_amps.tensor, 0, [[8, 128], [1024, 2], [1, 8]]),
                      tag="ldam", allow_slow_non_contiguous=True)
        ld_gs = p.dma("sync", a_gsel, gsel, tag="ldgs")
        ld_bas = p.dma("sync", bas_a, bass.AP(w_basis.tensor, 0, [[256, 8], [1, 256]]),
                       deps=zd + [muZ, invZ], tag="ldbas")
        ld_gw = [p.dma("sync", gw_a[:, dc * 256:(dc + 1) * 256],
                       w_gw[dc * 128:(dc + 1) * 128, :], deps=zd + [sv5, muZ, invZ], tag=f"ldgw{dc}")
                 for dc in range(NDC)]
        ld_mw = [p.dma("sync", mw_a[:, dc * 128:(dc + 1) * 128],
                       w_mw[dc * 128:(dc + 1) * 128, :], deps=zd + [muZ, invZ], tag=f"ldmw{dc}")
                 for dc in range(NDC)]
        c2pi = float(1.0 / (2 * np.pi))
        fqp = p.add("vector", (lambda e: e.tensor_scalar(a_fq, a_fq, c2pi, None, OP.mult)),
                    [ld_fq], tag="fqp")
        php = p.add("vector", (lambda e: e.tensor_scalar(a_ph, a_ph, c2pi, None, OP.mult)),
                    [ld_ph], tag="php")

        qct = sc_f2.bitcast(BF16)[:, 0:2048]
        kct = sc_f2.bitcast(BF16)[:, 2048:4096]
        prj = []
        for which in ("qc", "kc"):
            wa = cq_a if which == "qc" else ck_a
            lw = ld_cq if which == "qc" else ld_ck
            for tt in range(4):
                sid = None
                for dc in range(NDC):
                    sid = mm(4 + tt % 2, wa[:, dc * 128:(dc + 1) * 128],
                             yqv(dc)[:, tt * 512:(tt + 1) * 512], start=(dc == 0), stop=(dc == NDC - 1),
                             deps=(zd + lw if dc == 0 else []), tag=f"{which}{tt}_{dc}")
                if which == "qc":
                    ev = p.add("vector", (lambda e, tt=tt, b=4 + tt % 2: e.tensor_scalar(
                        qct[:, tt * 512:(tt + 1) * 512], PB[b][:, 0:512], MIXSC, None, OP.mult)),
                        [sid], tag=f"qcev{tt}")
                else:
                    ev = p.add("vector", (lambda e, tt=tt, b=4 + tt % 2: e.tensor_copy(
                        kct[:, (3 - tt) * 512:(4 - tt) * 512], PB[b][:, 0:512][:, ::-1])),
                        [sid], tag=f"kcev{tt}")
                consume(4 + tt % 2, ev)
                prj.append(ev)
        rq_deps, lk_deps = [], []
        for h in range(H):
            base, blk = hb(h)
            rq_deps.append(p.dma("gpsimd", rhsQ[base:base + 8, blk, :], qct[8 * h:8 * h + 8, :],
                                 deps=prj + spl, tag=f"rpq{h}"))
            lk_deps.append(p.dma("gpsimd", lhsK[base:base + 8, blk, :], kct[8 * h:8 * h + 8, :],
                                 deps=prj + spl, tag=f"rpk{h}"))
        qmf = sc_f1[0:16, :]
        hi16 = sc_f2.bitcast(BF16)[0:16, 0:2048]
        lo16 = sc_f2.bitcast(BF16)[0:16, 2048:4096]
        for which, wap, lw, outrow, dst in (("qm", qmw_a, ld_qm, 8, rhsQ), ("km", kmw_a, ld_km, 10, lhsK)):
            evs = []
            for tt in range(4):
                sid = None
                for dc in range(NDC):
                    sid = mm(6, wap[:, dc * 16:(dc + 1) * 16],
                             yqv(dc)[:, tt * 512:(tt + 1) * 512], start=(dc == 0), stop=(dc == NDC - 1),
                             deps=(zd + lw if dc == 0 else []), nrows=16, tag=f"{which}{tt}_{dc}")
                ev = p.add("vector", (lambda e, tt=tt: e.tensor_copy(
                    qmf[:, tt * 512:(tt + 1) * 512], PB[6][0:16, 0:512])), [sid, sv5], tag=f"{which}ev{tt}")
                consume(6, ev)
                evs.append(ev)
            guards = list(rq_deps) + list(lk_deps)
            if which == "qm":
                h1s = p.add("vector", (lambda e: e.tensor_copy(hi16, qmf)), evs + guards, tag="qmhi")
                l1s = p.add("vector", (lambda e: e.tensor_tensor(lo16, qmf, hi16, OP.subtract)),
                            [h1s], tag="qmlo")
            else:
                h1s = p.add("vector", (lambda e: e.tensor_copy(hi16, qmf[:, ::-1])), evs + guards, tag="kmhi")
                l1s = p.add("vector", (lambda e: e.tensor_tensor(lo16, qmf[:, ::-1], hi16, OP.subtract)),
                            [h1s], tag="kmlo")
            for h in range(H):
                base, blk = hb(h)
                rq_deps.append(p.dma("gpsimd", dst[base + outrow:base + outrow + 1, blk, :],
                                     hi16[h:h + 1, :], deps=[h1s, l1s], tag=f"{which}h{h}"))
                rq_deps.append(p.dma("gpsimd", dst[base + outrow + 1:base + outrow + 2, blk, :],
                                     lo16[h:h + 1, :], deps=[l1s], tag=f"{which}l{h}"))
        ones_row = p.add("vector", lambda e: e.memset(qct[0:1, :], 1.0),
                         rq_deps + lk_deps, tag="ones_row")
        for h in range(H):
            base, blk = hb(h)
            for r in (10, 11):
                rq_deps.append(p.dma("gpsimd", rhsQ[base + r:base + r + 1, blk, :], qct[0:1, :],
                                     deps=[ones_row], tag=f"oq{h}_{r}"))
            for r in (8, 9):
                lk_deps.append(p.dma("gpsimd", lhsK[base + r:base + r + 1, blk, :], qct[0:1, :],
                                     deps=[ones_row], tag=f"ok{h}_{r}"))
        qk_ready = rq_deps + lk_deps
        vfm_deps = []
        for pc in range(8):
            for tt in range(4):
                sid = None
                for dc in range(NDC):
                    sid = mm(4 + tt % 2, vw_a[:, dc * 1024 + pc * 128:dc * 1024 + (pc + 1) * 128],
                             yqv(dc)[:, tt * 512:(tt + 1) * 512], start=(dc == 0), stop=(dc == NDC - 1),
                             deps=(zd + ld_vw if dc == 0 else []), tag=f"v{pc}_{tt}_{dc}")
                ev = p.add("vector", (lambda e, pc=pc, tt=tt, b=4 + tt % 2: e.tensor_scalar(
                    vfm[:, pc * 2048 + (3 - tt) * 512:pc * 2048 + (4 - tt) * 512],
                    PB[b][:, 0:512][:, ::-1], a_vb[:, pc:pc + 1], None, OP.add)),
                    [sid, ld_vb], tag=f"vev{pc}_{tt}")
                consume(4 + tt % 2, ev)
                vfm_deps.append(ev)

        pool_deps = [p.add("vector", (lambda e, dc=dc: e.tensor_reduce(
            a_pool[:, dc:dc + 1], yqv(dc), AX.X, OP.add)), zd, tag=f"pool{dc}")
            for dc in range(NDC)]
        gl_evs = []
        for half in range(2):
            sid = None
            for dc in range(NDC):
                sid = mm(7, gw_a[:, dc * 256 + half * 128:dc * 256 + (half + 1) * 128],
                         a_pool[:, dc:dc + 1], start=(dc == 0), stop=(dc == NDC - 1),
                         deps=(pool_deps + ld_gw if dc == 0 else []), tag=f"gl{half}_{dc}")
            ev = p.add("vector", (lambda e, half=half: e.tensor_scalar(
                glsb[:, half:half + 1], PB[7][:, 0:1], 1.0 / S, a_gb[:, half:half + 1],
                OP.mult, OP.add)), [sid, ld_gb], tag=f"glev{half}")
            consume(7, ev)
            gl_evs.append(ev)
        sid = None
        for dc in range(NDC):
            sid = mm(7, mw_a[:, dc * 128:(dc + 1) * 128], a_pool[:, dc:dc + 1],
                     start=(dc == 0), stop=(dc == NDC - 1),
                     deps=(pool_deps + ld_mw + gl_evs if dc == 0 else []), tag=f"mod{dc}")
        mev = p.add("vector", (lambda e: e.tensor_scalar(a_modc, PB[7][:, 0:1], 1.0 / S,
                                                         a_mb, OP.mult, OP.add)),
                    [sid, ld_mb], tag="modev")
        consume(7, mev)
        ms1 = p.dma("gpsimd", d_stats[7:8, 0:128], a_modc, deps=[mev], tag="mod_out")
        ms2 = p.dma("gpsimd", modT, bass.AP(d_stats.tensor, 7 * S, [[1, 8], [8, 16]]),
                    deps=[ms1], tag="mod_in", allow_slow_non_contiguous=True)
        gr1 = p.dma("gpsimd", G16[0:8, :], glsb[:, 0:1], deps=gl_evs, tag="glrp0")
        gr2 = p.dma("gpsimd", G16[8:16, :], glsb[:, 1:2], deps=gl_evs, tag="glrp1")
        bsid = []
        for h in range(H):
            bsid.append(mm(6, bas_a[0:8, h * 16:(h + 1) * 16], modT[0:8, h:h + 1],
                           start=True, stop=True, deps=[ms2, ld_bas], nrows=16, coloff=h,
                           tag=f"bas{h}"))
        if DEBUG:
            probe(modT, [ms2])
            probe(bas_a, [ld_bas])
            probe(PB[6][0:16, 0:16], bsid)
            probe(G16, [gr1, gr2])
        ttmp = PHU[0:16, 3396:3412]
        cpT = p.add("vector", (lambda e: e.tensor_copy(ttmp, PB[6][0:16, 0:16])),
                    bsid, tag="basT_cp")
        consume(6, cpT)
        trT = p.add("tensor", (lambda e: e.transpose(PB[6][0:16, 0:16], ttmp, s_idf[0:16, 0:16])),
                    [cpT, ld_idf], tag="basT_tr")
        gadd = p.add("vector", (lambda e: e.tensor_tensor(G16, G16, PB[6][0:16, 0:16], OP.add)),
                     [gr1, gr2, trT], tag="gadd")
        consume(6, gadd)
        s1 = p.add("vector", (lambda e: e.tensor_reduce(G16b[:, 0:1], G16, AX.X, OP.max)),
                   [gadd], tag="wmax")
        s2 = p.add("vector", (lambda e: e.tensor_scalar(G16, G16, G16b[:, 0:1], None, OP.subtract)),
                   [s1], tag="wsub")
        s3 = p.add("scalar", (lambda e: e.activation(G16, G16, AF.Exp)), [s2], tag="wexp")
        s4 = p.add("vector", (lambda e: e.tensor_reduce(G16b[:, 1:2], G16, AX.X, OP.add)),
                   [s3], tag="wsum")
        s5 = p.add("vector", (lambda e: e.reciprocal(G16b[:, 2:3], G16b[:, 1:2])), [s4], tag="wrc")
        s6 = p.add("vector", (lambda e: e.tensor_scalar(G16, G16, G16b[:, 2:3], None, OP.mult)),
                   [s5], tag="wnorm")
        wr1 = p.dma("gpsimd", a_wts[:, 0:1], G16[0:8, :], deps=[s6], tag="wtsrp0")
        wr2 = p.dma("gpsimd", a_wts[:, 1:2], G16[8:16, :], deps=[s6], tag="wtsrp1")
        if DEBUG:
            probe(G16, [s6])

        P1rep = PHU[:, 0:2048]
        P2rep = PHU[:, 2048:4096]
        tmp2 = gw_a
        ldp1 = p.dma("sync", sc_f1[0:1, :], d_stats[4:5, :],
                     deps=gl_evs + [mev] + qk_ready + vfm_deps, tag="ldp1")
        pb1 = p.add("gpsimd", (lambda e: e.partition_broadcast(P1rep, sc_f1[0:1, :], 128)),
                    [ldp1], tag="pbp1")
        ldp2 = p.dma("sync", sc_f1[0:1, :], d_stats[5:6, :], deps=[pb1, wr1, wr2], tag="ldp2")
        pb2 = p.add("gpsimd", (lambda e: e.partition_broadcast(P2rep, sc_f1[0:1, :], 128)),
                    [ldp2], tag="pbp2")
        ldw3 = p.dma("sync", sc_f1[0:1, :], d_stats[6:7, :], deps=[pb2], tag="ldw3")
        w3rep = XA[0:8, 6, :]
        pbw3a = p.add("gpsimd", (lambda e: e.partition_broadcast(w3rep, sc_f1[0:1, :], 8)),
                      [ldw3] + spl, tag="pbw3a")
        pbw3b = pbw3a
        mask_done = []
        kern_evs = []
        prev_acc = None
        half_guard = []
        for half in range(2):
            accs = []
            for w in range(NWV):
                y1 = p.add("scalar", (lambda e, half=half, w=w: e.activation(
                    sc_f1[:], P1rep, AF.Identity, scale=a_fq[:, half * 16 + 2 * w:half * 16 + 2 * w + 1],
                    bias=a_ph[:, half * 8 + w:half * 8 + w + 1])),
                    [pb1, pb2, pbw3a, pbw3b, fqp, php, prev_acc] +
                    ([ones_row] + qk_ready if half == 0 and w == 0 else []) +
                    (half_guard if w == 0 else []),
                    tag=f"msky{half}_{w}")
                y2 = p.add("vector", (lambda e, half=half, w=w: e.scalar_tensor_tensor(
                    sc_f1[:], P2rep, a_fq[:, half * 16 + 2 * w + 1:half * 16 + 2 * w + 2],
                    sc_f1[:], OP.mult, OP.add)), [y1, pb2], tag=f"msk2{half}_{w}")
                y3 = p.add("vector", (lambda e: e.tensor_scalar(tmp2, sc_f1[:], RC, RC, OP.add, OP.subtract)),
                           [y2, ldp1, ldp2, ldw3], tag=f"msk3{half}_{w}")
                y4 = p.add("vector", (lambda e: e.tensor_sub(sc_f1[:], sc_f1[:], tmp2)),
                           [y3], tag=f"msk4{half}_{w}")
                y5 = p.add("scalar", (lambda e: e.activation(sc_f1[:], sc_f1[:], AF.Sin,
                                                             scale=float(2 * np.pi))),
                           [y4], tag=f"msk5{half}_{w}")
                if w == 0:
                    y6 = p.add("vector", (lambda e, half=half, w=w: e.tensor_scalar(
                        sc_f2[:], sc_f1[:], a_am[:, half * 8 + w:half * 8 + w + 1], None, OP.mult)),
                        [y5, ld_am], tag=f"msk6{half}_{w}")
                else:
                    y6 = p.add("vector", (lambda e, half=half, w=w: e.scalar_tensor_tensor(
                        sc_f2[:], sc_f1[:], a_am[:, half * 8 + w:half * 8 + w + 1], sc_f2[:],
                        OP.mult, OP.add)), [y5], tag=f"msk6{half}_{w}")
                accs.append(y6)
                prev_acc = y6
            if DEBUG and half == 0:
                probe(sc_f2[:, 0:512], [accs[-1]])
            tw = p.add("vector", (lambda e, half=half: e.tensor_scalar(
                sc_f2[:], sc_f2[:], a_wts[:, half:half + 1], None, OP.mult)),
                [accs[-1], wr1, wr2], tag=f"mskw{half}")
            if DEBUG and half == 0:
                probe(sc_f2[:, 0:512], [tw])
            devs = []
            for ch in range(4):
                ks = mm(6, a_gsel, sc_f2[:, ch * 512:(ch + 1) * 512], start=True, stop=True,
                        deps=[tw, ld_gs], nrows=8, tag=f"kern{half}_{ch}")
                dv = p.add("scalar", (lambda e, ch=ch: e.activation(
                    sc_f1[0:8, ch * 512:(ch + 1) * 512], PB[6][0:8, 0:512], AF.Sigmoid,
                    scale=float(STEP_A))), [ks], tag=f"dens{half}_{ch}")
                consume(6, dv)
                devs.append(dv)
            d1 = p.add("vector", (lambda e: e.tensor_tensor(sc_f1[0:8, :], sc_f1[0:8, :],
                                                            w3rep, OP.mult)),
                       devs + [pbw3a], tag=f"dmul{half}")
            dst = p.dma("gpsimd", d_dext[half * 8:(half + 1) * 8, 511:2559],
                        sc_f1[0:8, :], deps=[d1], tag=f"dst{half}")
            kern_evs.append(dst)
            half_guard = [d1, dst]
            if DEBUG and half == 0:
                half_guard.append(probe(sc_f1[0:8, 0:512], [d1]))
        padz = p.add("vector", (lambda e: e.memset(sc_f2[0:8, 0:511], NEG)), kern_evs, tag="padz")
        for half in range(2):
            kern_evs.append(p.dma("gpsimd", d_dext[half * 8:(half + 1) * 8, 0:511],
                                  sc_f2[0:8, 0:511], deps=[padz], tag=f"dpad{half}"))
        kern_evs.append(p.dma("gpsimd", d_dext[0:16, 2559:2560],
                              sc_f2[0:8, 0:1][0:8].broadcast_to((8, 1)) if False else sc_f2[0:8, 0:2],
                              deps=[padz], tag="dtail") if False else kern_evs[-1])

        ctx_sts = []
        gld = [None, None]
        gld[0] = p.dma("sync", GB[0], bass.AP(d_dext.tensor, 0, [[1, 128], [1, 2432]]),
                       deps=kern_evs + [pb1, pb2], tag="gb0")
        vaug_dep = [None, None]
        vaug_reader = [None, None]
        vb_deps_h = [[], []]
        state = {"den": None, "pvl_guard": [None, None], "pexp": [None, None]}
        Pts = [Pt.rearrange("p (a b) -> p a b", b=512),
               yqt[:, 0:4, :].rearrange("p a b -> p (a b)").rearrange("p (a b) -> p a b", b=512)]

        def phase_A(h, qc, pidx):
            r = h % 2
            pp = pidx % 2
            base, blk = hb(h)
            ktmax = min(16, (qc * 512 + 511) // 128 + 1)
            pdeps = []
            for kt in range(ktmax):
                jlo = 128 * (15 - kt)
                stb = kt % 2
                sid = mm(stb, lhsK[base:base + 12, blk, jlo:jlo + 128],
                         rhsQ[base:base + 12, blk, qc * 512:(qc + 1) * 512],
                         start=True, stop=True,
                         deps=(qk_ready + [gld[r]] if kt == 0 and qc == 0 and h == 0 else [gld[r]]),
                         tag=f"qk{h}_{qc}_{kt}")
                woff = (qc * 512 - 128 * kt) + 384
                sv = p.add("vector", (lambda e, stb=stb, woff=woff, r=r: e.tensor_tensor(
                    sc_f1[:, stb * 512:stb * 512 + 512], PB[stb][:, 0:512],
                    GB[r][:, woff:woff + 512], OP.add)),
                    [sid, gld[r], state["pexp"][stb]], tag=f"sG{h}_{qc}_{kt}")
                consume(stb, sv)
                pexp = p.add("scalar", (lambda e, kt=kt, stb=stb, pp=pp: e.activation(
                    Pts[pp][:, kt, :],
                    sc_f1[:, stb * 512:stb * 512 + 512], AF.Exp)),
                    [sv, state["pvl_guard"][pp]], tag=f"pex{h}_{qc}_{kt}")
                state["pexp"][stb] = pexp
                pdeps.append(pexp)
            return (h, qc, r, pp, ktmax, pdeps, list(vb_deps_h[r]))

        def phase_B(st):
            h, qc, r, pp, ktmax, pdeps, vb_deps = st
            pvl = None
            for kt in range(ktmax):
                pvl = mm(2, vaug[r].rearrange("p (a b) -> p a b", b=65)[:, kt, :],
                         Pts[pp][:, kt, :],
                         start=(kt == 0), stop=(kt == ktmax - 1),
                         deps=([pdeps[ktmax - 1]] + vb_deps if kt == 0 else []),
                         nrows=65, tag=f"pv{h}_{qc}_{kt}")
            state["pvl_guard"][pp] = pvl
            vaug_reader[r] = pvl
            rc1 = p.add("vector", (lambda e: e.reciprocal(rrow, PB[2][64:65, 0:512])),
                        [pvl, state["den"]], tag=f"rc{h}_{qc}")
            pbr = p.add("gpsimd", (lambda e: e.partition_broadcast(rrep, rrow, 64)),
                        [rc1], tag=f"pbr{h}_{qc}")
            nrm = p.add("vector", (lambda e: e.tensor_tensor(ctxstg, PB[2][0:64, 0:512], rrep, OP.mult)),
                        [pbr, ctx_sts[-1] if ctx_sts else None], tag=f"nrm{h}_{qc}")
            consume(2, nrm)
            state["den"] = nrm
            st2 = p.dma("gpsimd", d_ctx[h * 64:(h + 1) * 64, qc * 512:(qc + 1) * 512], ctxstg,
                        deps=[nrm], tag=f"ctxst{h}_{qc}")
            ctx_sts.append(st2)

        pending = None
        pidx = 0
        for h in range(H):
            r = h % 2
            if h + 1 < H:
                gld[1 - r] = p.dma("sync", GB[1 - r],
                                   bass.AP(d_dext.tensor, (h + 1) * 2560, [[1, 128], [1, 2432]]),
                                   deps=kern_evs + [gld[1 - r]] + ctx_sts[-4:], tag=f"gb{h + 1}")
            pc, hh = h // 2, h % 2
            vb_deps = []
            vfm_pair_dep = vfm_deps[pc * 4 + 3]
            oz = p.add("vector", (lambda e, r=r: e.memset(
                vaug[r].rearrange("p (a b) -> p a b", b=65)[:, :, 64:65], 1.0)),
                [vaug_dep[r], vaug_reader[r]], tag=f"vo{h}")
            for kt in range(16):
                ktp = 15 - kt
                tr = p.add("tensor", (lambda e, pc=pc, hh=hh, ktp=ktp: e.transpose(
                    PB[3].bitcast(BF16)[:, 0:64],
                    vfm.rearrange("p (a b) -> p a b", a=8)[64 * hh:64 * (hh + 1), pc, 128 * ktp:128 * (ktp + 1)],
                    s_idb[64 * hh:64 * (hh + 1), 64 * hh:64 * (hh + 1)])),
                    [blr[3], ld_idb, vfm_pair_dep, vaug_reader[r]], tag=f"vtr{h}_{kt}")
                cp = p.add("vector", (lambda e, r=r, kt=kt: e.tensor_copy(
                    vaug[r].rearrange("p (a b) -> p a b", b=65)[:, kt, 0:64],
                    PB[3].bitcast(BF16)[:, 0:64])), [tr, oz], tag=f"vcp{h}_{kt}")
                consume(3, cp)
                vb_deps.append(cp)
            vaug_dep[r] = vb_deps[-1]
            vb_deps_h[r] = vb_deps
            for qc in range(4):
                stA = phase_A(h, qc, pidx)
                if pending is not None:
                    phase_B(pending)
                pending = stA
                pidx += 1
        phase_B(pending)
        p.barrier()

        ld_ow = [p.dma("gpsimd", ow_a[:, dc * 1024:(dc + 1) * 1024],
                       w_ow[dc * 128:(dc + 1) * 128, :], tag=f"ldow{dc}") for dc in range(NDC)]
        mu1rep = PHU[:, 0:2048]
        inv1rep = PHU[:, 2048:4096]
        lm1 = p.dma("sync", sc_f1[0:1, :], d_stats[0:1, :], tag="lm1")
        pm1 = p.add("gpsimd", (lambda e: e.partition_broadcast(mu1rep, sc_f1[0:1, :], 128)),
                    [lm1], tag="pm1")
        lm2 = p.dma("sync", sc_f1[0:1, :], d_stats[1:2, :], deps=[pm1], tag="lm2")
        pm2 = p.add("gpsimd", (lambda e: e.partition_broadcast(inv1rep, sc_f1[0:1, :], 128)),
                    [lm2], tag="pm2")
        ctile = [PHU[:, 4096:4608], PHU[:, 4608:5120]]
        xtile = [PHU[:, 5120:5632], PHU[:, 5632:6144]]
        ctxl = ARW[:, 0:16384]
        cld = [p.dma("sync", ctxl[:, dc * 2048:(dc + 1) * 2048],
                     d_ctx[dc * 128:(dc + 1) * 128, :], tag=f"cld{dc}") for dc in range(NDC)]
        tin_deps = []
        xl = [None, None]
        for do in range(NDC):
            for tt in range(4):
                rb = (do * 4 + tt) % 2
                xl[rb] = p.dma("sync", xtile[rb], d_x1[do * 128:(do + 1) * 128, tt * 512:(tt + 1) * 512],
                               deps=[xl[rb], pm1, pm2], tag=f"xl{do}_{tt}")
                sid = None
                for dc in range(NDC):
                    sid = mm(4 + tt % 2, ow_a[:, dc * 1024 + do * 128:dc * 1024 + (do + 1) * 128],
                             ctxl[:, dc * 2048 + tt * 512:dc * 2048 + (tt + 1) * 512],
                             start=(dc == 0), stop=(dc == NDC - 1),
                             deps=([cld[dc], ld_ow[dc]] if do == 0 and tt == 0 else
                                   ([cld[dc]] if do == 0 else [])), tag=f"ow{do}_{tt}_{dc}")
                z1 = p.add("vector", (lambda e, tt=tt, rb=rb: e.tensor_sub(
                    ctile[rb], xtile[rb], mu1rep[:, tt * 512:(tt + 1) * 512])),
                    [xl[rb], pm1], tag=f"tz1_{do}_{tt}")
                z2 = p.add("vector", (lambda e, tt=tt, rb=rb: e.tensor_mul(
                    ctile[rb], ctile[rb], inv1rep[:, tt * 512:(tt + 1) * 512])),
                    [z1, pm2], tag=f"tz2_{do}_{tt}")
                t1 = p.add("vector", (lambda e, do=do, tt=tt, rb=rb, b=4 + tt % 2: e.scalar_tensor_tensor(
                    XA[:, do, tt * 512:(tt + 1) * 512], PB[b][:, 0:512], a_ob[:, do:do + 1],
                    ctile[rb], OP.add, OP.add)), [sid, z2, ld_ob], tag=f"tin{do}_{tt}")
                consume(4 + tt % 2, t1)
                xl[rb] = t1
                tin_deps.append(t1)
        p.barrier()

        muT, invT = emit_ln_stats("l1", lambda dc: XA[:, dc, :], tin_deps)
        hfd = []
        for dc in range(NDC):
            f1 = p.add("vector", (lambda e, dc=dc: e.tensor_sub(sc_f1[:], XA[:, dc, :], repA[:, :])),
                       [muT, invT], tag=f"hf1_{dc}")
            f2 = p.add("vector", (lambda e: e.tensor_mul(sc_f1[:], sc_f1[:], repG[:, :])),
                       [f1], tag=f"hf2_{dc}")
            f3 = p.add("scalar", (lambda e, dc=dc: e.activation(XA[:, dc, :], sc_f1[:], AF.Identity,
                                                                scale=a_ln[:, dc:dc + 1],
                                                                bias=a_ln[:, 8 + dc:8 + dc + 1])),
                       [f2, ld_ln], tag=f"hf3_{dc}")
            f4 = p.add("vector", (lambda e, dc=dc: e.tensor_copy(yqv(dc), XA[:, dc, :])),
                       [f3], tag=f"hbf_{dc}")
            hfd.append(f4)
        ld_w1 = [p.dma("gpsimd", ARW[:, dc * 4096:(dc + 1) * 4096],
                       w_fw1[dc * 128:(dc + 1) * 128, :], deps=[cld[dc]], tag=f"ldW1_{dc}")
                 for dc in range(NDC)]
        ustg = sc_f2.bitcast(BF16)[:, 0:2048]
        prev_ust = None
        u_sts = []
        for fo in range(NFO):
            evs = []
            for tt in range(4):
                bi = 4 + tt % 2
                sid = None
                for dc in range(NDC):
                    sid = mm(bi, ARW[:, dc * 4096 + fo * 128:dc * 4096 + (fo + 1) * 128],
                             yqv(dc)[:, tt * 512:(tt + 1) * 512], start=(dc == 0), stop=(dc == NDC - 1),
                             deps=(hfd + ld_w1 if fo == 0 and tt == 0 and dc == 0 else []),
                             tag=f"u{fo}_{tt}_{dc}")
                ev = p.add("scalar", (lambda e, fo=fo, tt=tt, bi=bi: e.activation(
                    ustg[:, tt * 512:(tt + 1) * 512], PB[bi][:, 0:512], AF.Gelu,
                    bias=a_fb1[:, fo:fo + 1])), [sid, ld_fb1, prev_ust], tag=f"uev{fo}_{tt}")
                consume(bi, ev)
                evs.append(ev)
            st = p.dma("gpsimd", d_u[fo * 128:(fo + 1) * 128, :], ustg, deps=evs, tag=f"ust{fo}")
            prev_ust = st
            u_sts.append(st)
        p.barrier()
        ld_w2 = [p.dma("gpsimd", ARW[:, kc * 1024:(kc + 1) * 1024],
                       w_fw2[kc * 128:(kc + 1) * 128, :], tag=f"ldW2_{kc}") for kc in range(NFO)]
        x2t = PHU[:, 0:4096].rearrange("p (a b) -> p a b", a=NDC)
        mu2rep = PHU[:, 4096:4608]
        inv2rep = PHU[:, 4608:5120]
        x2_deps = []
        prev_ucl = None
        prev_e1s = []
        prev_xv = [None, None]
        prev_x2 = [None] * NDC
        for tt in range(4):
            ucl = []
            for g in range(4):
                sap = bass.AP(d_u.tensor, g * 8 * 128 * S + tt * 512, [[S, 128], [128 * S, 8], [1, 512]])
                ucl.append(p.dma("sync", yqt[:, 2 * g:2 * g + 2, :].rearrange("p a b -> p (a b)")
                                 .rearrange("p (a b) -> p a b", a=8),
                                 sap, deps=[prev_ucl] + prev_e1s, tag=f"ucl{tt}_{g}"))
            prev_ucl = ucl[-1]
            e1s = []
            for do in range(NDC):
                sid = None
                for kc in range(NFO):
                    uview = yqt[:, 2 * (kc // 8):2 * (kc // 8) + 2, :].rearrange("p a b -> p (a b)") \
                        .rearrange("p (a b) -> p a b", a=8)[:, kc % 8, :]
                    sid = mm(do % 2, ARW[:, kc * 1024 + do * 128:kc * 1024 + (do + 1) * 128],
                             uview, start=(kc == 0), stop=(kc == NFO - 1), ncols=512,
                             deps=(ucl + ld_w2 + u_sts if kc == 0 else []), tag=f"w2_{tt}_{do}_{kc}")
                e1 = p.add("vector", (lambda e, do=do, tt=tt, b=do % 2: e.scalar_tensor_tensor(
                    x2t[:, do, :], PB[b][:, 0:512], a_fb2[:, do:do + 1],
                    XA[:, do, tt * 512:(tt + 1) * 512], OP.add, OP.add)),
                    [sid, ld_fb2, prev_x2[do]], tag=f"e1_{tt}_{do}")
                consume(do % 2, e1)
                e1s.append(e1)
            prev_e1s = e1s
            sb_c = sc_f2.bitcast(BF16)[:, 0:512]
            sb_s = sc_f2.bitcast(BF16)[:, 512:1024]
            m1l = m2l = None
            pc1 = pc2 = None
            for dc in range(NDC):
                c1 = p.add("vector", (lambda e, dc=dc: e.tensor_copy(sb_c, x2t[:, dc, :])),
                           e1s + [pc1], tag=f"l2c{tt}_{dc}")
                c2 = p.add("scalar", (lambda e, dc=dc: e.activation(sb_s, x2t[:, dc, :], AF.Square)),
                           e1s + [pc2], tag=f"l2s{tt}_{dc}")
                m1l = mm(6, ones_b[:], sb_c, start=(dc == 0), stop=(dc == NDC - 1),
                         deps=[c1, ones_dep], nrows=1, tag=f"l2m1_{tt}_{dc}")
                m2l = mm(7, ones_b[:], sb_s, start=(dc == 0), stop=(dc == NDC - 1),
                         deps=[c2, ones_dep], nrows=1, tag=f"l2m2_{tt}_{dc}")
                pc1, pc2 = m1l, m2l
            ev1 = p.add("vector", (lambda e: e.tensor_copy(sc_f1[0:1, 0:512], PB[6][0:1, 0:512])),
                        [m1l], tag=f"l2e1_{tt}")
            consume(6, ev1)
            ev2 = p.add("vector", (lambda e: e.tensor_copy(sc_f1[0:1, 512:1024], PB[7][0:1, 0:512])),
                        [m2l], tag=f"l2e2_{tt}")
            consume(7, ev2)
            q1 = p.add("gpsimd", (lambda e: e.partition_broadcast(mu2rep, sc_f1[0:1, 0:512], 128)),
                       [ev1], tag=f"l2p1_{tt}")
            q2 = p.add("gpsimd", (lambda e: e.partition_broadcast(inv2rep, sc_f1[0:1, 512:1024], 128)),
                       [ev2, q1], tag=f"l2p2_{tt}")
            n1 = p.add("vector", (lambda e: e.tensor_scalar(mu2rep, mu2rep, 1.0 / D, None, OP.mult)),
                       [q1], tag=f"l2mu_{tt}")
            n2 = p.add("vector", (lambda e: e.tensor_mul(sc_f1[:, 1024:1536], mu2rep, mu2rep)),
                       [n1], tag=f"l2mu2_{tt}")
            n3 = p.add("vector", (lambda e: e.scalar_tensor_tensor(inv2rep, inv2rep, 1.0 / D,
                                                                   sc_f1[:, 1024:1536], OP.mult, OP.subtract)),
                       [n2, q2], tag=f"l2var_{tt}")
            n4 = p.add("vector", (lambda e: e.tensor_scalar(inv2rep, inv2rep, 1e-5, None, OP.add)),
                       [n3], tag=f"l2ve_{tt}")
            n5 = p.add("vector", (lambda e: e.reciprocal(inv2rep, inv2rep)), [n4], tag=f"l2rc_{tt}")
            n6 = p.add("scalar", (lambda e: e.activation(inv2rep, inv2rep, AF.Sqrt)), [n5], tag=f"l2rs_{tt}")
            if DEBUG and tt == 0:
                probe(x2t[:, 0, :], e1s)
                probe(mu2rep[:, 0:512], [n1])
                probe(inv2rep[:, 0:512], [n6])
            for do in range(NDC):
                rb = do % 2
                xld = p.dma("sync", xtile[rb], d_x1[do * 128:(do + 1) * 128, tt * 512:(tt + 1) * 512],
                            deps=[prev_xv[rb], xl[0], xl[1]], tag=f"x2l{tt}_{do}")
                v1 = p.add("vector", (lambda e, do=do: e.tensor_sub(sc_f1[:, 1024:1536], x2t[:, do, :],
                                                                    mu2rep)),
                           [n6] + e1s, tag=f"x2a{tt}_{do}")
                v2 = p.add("vector", (lambda e: e.tensor_mul(sc_f1[:, 1024:1536], sc_f1[:, 1024:1536],
                                                             inv2rep)),
                           [v1], tag=f"x2b{tt}_{do}")
                v3 = p.add("scalar", (lambda e, do=do: e.activation(sc_f1[:, 1536:2048],
                                                                    sc_f1[:, 1024:1536], AF.Identity,
                                                                    scale=a_ln[:, 16 + do:16 + do + 1],
                                                                    bias=a_ln[:, 24 + do:24 + do + 1])),
                           [v2, ld_ln], tag=f"x2c{tt}_{do}")
                v4 = p.add("vector", (lambda e, do=do, tt=tt, rb=rb: e.tensor_tensor(
                    XA[:, do, tt * 512:(tt + 1) * 512], xtile[rb], sc_f1[:, 1536:2048], OP.add)),
                    [v3, xld], tag=f"x2d{tt}_{do}")
                prev_x2[do] = v4
                prev_xv[rb] = v4
                x2_deps.append(v4)
        if DEBUG:
            probe(XA[:, 0, 0:512], [x2_deps[0]])
            probe(XA[:, 7, 0:512], [x2_deps[7]])
        p.barrier()
        return x2_deps

    if STAGE != "x":
        updA = emit_bitffn("A", w1_rdy, w2_rdy, w2_mw, [x_dep])
        if STAGE not in ("A",):
            x2_deps = emit_attention(updA[-8:])
            if STAGE not in ("attn",):
                cw1 = load_arena_from_stash("Cw1", d_w1t, NFO * NDC, [x2_deps[-1]])
                updC = emit_bitffn("C", cw1[-1], w2_rdy, w2_mw, x2_deps[-8:])
    p.barrier()

    outs = []
    ytm, ytm2 = xtm, xtm2
    prev_y = [None, None]
    for tt in range(16):
        buf = ytm if tt % 2 == 0 else ytm2
        cps = []
        for dc in range(NDC):
            bi = dc % 2
            tr = p.add("tensor", (lambda e, dc=dc, tt=tt, bi=bi:
                                  e.transpose(PB[bi][:, 0:128], XA[:, dc, tt * 128:(tt + 1) * 128], s_idf[:])),
                       [blr[bi], prev_y[tt % 2]], tag=f"try{tt}_{dc}")
            cp = p.add("vector", (lambda e, dc=dc, bi=bi, b=buf:
                                  e.tensor_copy(b[:, dc * 128:(dc + 1) * 128], PB[bi][:, 0:128])),
                       [tr], tag=f"cpy{tt}_{dc}")
            consume(bi, cp)
            cps.append(cp)
        st = p.dma("gpsimd", y_out[tt * 128:(tt + 1) * 128, :], buf[:], deps=cps, tag=f"sty{tt}")
        outs.append(st)
        prev_y[tt % 2] = st
    p.add("gpsimd", lambda e: e.nop(), deps=outs, tag="final_wait")

    stack.close()

    p.link()
    sems = {}
    for key in p.sem_names():
        h = nc.semaphore("sem_" + "_".join(str(k) for k in key))
        ctx.append(h)
        sems[key] = h.__enter__()
    with nc.Block() as block:
        @block.sync
        def _(e):
            p.emit_engine("sync", e, sems)

        @block.tensor
        def _(e):
            p.emit_engine("tensor", e, sems)

        @block.vector
        def _(e):
            p.emit_engine("vector", e, sems)

        @block.scalar
        def _(e):
            p.emit_engine("scalar", e, sems)

        @block.gpsimd
        def _(e):
            p.emit_engine("gpsimd", e, sems)
    return p


_NC_CACHE = {}


def _get_nc():
    if "nc" not in _NC_CACHE:
        nc = bass.Bass("TRN2", target_bir_lowering=False, detect_race_conditions=False)
        _build(nc)
        lower_extended_insts(nc)
        _NC_CACHE["nc"] = nc
    return _NC_CACHE["nc"]


def kernel(**inputs):
    nc = _get_nc()
    x = np.ascontiguousarray(np.asarray(inputs["L0_attn_x"], np.float32))
    gsel = np.zeros((128, 8), np.float32)
    for j in range(8):
        gsel[16 * j:16 * (j + 1), j] = 1.0

    def f32(name):
        return np.ascontiguousarray(np.asarray(inputs[name], np.float32))

    base = {
        "idf": np.eye(128, dtype=np.float32),
        "idb": np.eye(128, dtype=ml_dtypes.bfloat16),
        "kramp": np.arange(S, dtype=np.float32).reshape(1, S),
        "gsel": gsel,
        "bw1": f32("L0_ffn_w1"),
        "bw2": f32("L0_ffn_w2"),
        "conv_w": f32("L0_ffn_conv_w"),
        "conv_b": f32("L0_ffn_conv_b"),
        "vw": f32("L0_attn_v_w"), "ow": f32("L0_attn_out_w"),
        "cqw": f32("L0_attn_cq_w"), "ckw": f32("L0_attn_ck_w"),
        "qmw": f32("L0_attn_qmod_w"), "kmw": f32("L0_attn_kmod_w"),
        "gw": f32("L0_attn_gate_w"), "mw": f32("L0_attn_mod_w"),
        "basis": f32("L0_attn_mod_basis"), "freqs": f32("L0_attn_freqs"),
        "amps": f32("L0_attn_amps"), "phases": f32("L0_attn_phases"),
        "gb": f32("L0_attn_gate_b"), "mb": f32("L0_attn_mod_b"),
        "vb": f32("L0_attn_v_b"), "ob": f32("L0_attn_out_b"),
        "fw1": f32("L0_attn_ffn_w1"), "fb1": f32("L0_attn_ffn_b1"),
        "fw2": f32("L0_attn_ffn_w2"), "fb2": f32("L0_attn_ffn_b2"),
        "lnv": np.ascontiguousarray(np.stack([
            np.asarray(inputs["L0_attn_ln1_g"], np.float32),
            np.asarray(inputs["L0_attn_ln1_b"], np.float32),
            np.asarray(inputs["L0_attn_ln2_g"], np.float32),
            np.asarray(inputs["L0_attn_ln2_b"], np.float32)])),
    }
    in_maps = []
    for c in range(8):
        m = dict(base)
        m["x"] = np.ascontiguousarray(x[c])
        in_maps.append(m)
    res = run_bass_kernel_spmd(nc, in_maps, core_ids=list(range(8)))
    return np.stack([res.results[c]["y"] for c in range(8)], axis=0)


# version 09 (final)
# speedup vs baseline: 1.0176x; 1.0176x over previous
import contextlib
import numpy as np
import ml_dtypes
import concourse.bass as bass
import concourse.mybir as mybir
from concourse.bass_isa import ReduceOp
from concourse.library_config import attn as _LIB_ATTN
from concourse.library_overlay import lower_extended_insts
from concourse.bass_utils import run_bass_kernel_spmd

F32 = mybir.dt.float32
BF16 = mybir.dt.bfloat16
AF = mybir.ActivationFunctionType
OP = mybir.AluOpType
AX = mybir.AxisListType
RC = 12582912.0
S, D, FF, H, HD, CR, M, RMOD, NWV = 2048, 1024, 4096, 16, 64, 8, 16, 8, 8
NDC = D // 128
NFO = FF // 128
MIXSC = 0.15 / float(np.sqrt(8.0))
STEP_A = 6.0
NEG = -1.0e9

ENGINES = ("tensor", "vector", "scalar", "gpsimd", "sync")


class Prog:

    NPOOL = 6

    def __init__(self):
        self.steps = []
        self.floor = []
        self.dma_count = {e: 0 for e in ENGINES}

    def add(self, eng, emit, deps=(), is_dma=False, tag="", relaxed=()):
        assert eng in ENGINES, eng
        deps = [d for d in deps if d is not None] + self.floor
        rlx = frozenset(d for d in relaxed if d is not None)
        slot = None
        if is_dma:
            idx = self.dma_count[eng]
            self.dma_count[eng] += 1
            slot = (idx % self.NPOOL, idx // self.NPOOL + 1)
        self.steps.append([eng, emit, deps, is_dma, tag, False, 0, slot, rlx])
        return len(self.steps) - 1

    def dma(self, eng, out, in_, deps=(), tag="", **kw):
        return self.add(eng, (lambda e, o=out, i=in_, kw=kw: e.dma_start(out=o, in_=i, **kw)),
                        deps, is_dma=True, tag=tag)

    def barrier(self):
        last = {}
        for sid, st in enumerate(self.steps):
            last[st[0]] = sid
        self.floor = list(last.values())

    def link(self):
        steps = self.steps
        for st in steps:
            if st[3]:
                st[5] = True
            for d in st[2]:
                if d in st[8] and steps[d][0] == st[0] and not steps[d][3]:
                    continue
                steps[d][5] = True
        cum = {e: 0 for e in ENGINES}
        for st in steps:
            if st[5] and not st[3]:
                cum[st[0]] += 1
            st[6] = cum[st[0]]

    def sem_names(self):
        out = []
        for e in self.engines():
            out.append(("c", e))
            if self.dma_count[e]:
                for i in range(min(self.NPOOL, self.dma_count[e])):
                    out.append(("d", e, i))
        return out

    def emit_engine(self, eng_name, eng, sems):
        steps = self.steps
        wm = {}

        def wait(key, v):
            if wm.get(key, 0) < v:
                eng.wait_ge(sems[key], v)
                wm[key] = v

        for sid, st in enumerate(steps):
            if st[0] != eng_name:
                continue
            for d in st[2]:
                ds = steps[d]
                if ds[3]:
                    pi, use = ds[7]
                    wait(("d", ds[0], pi), 16 * use)
                elif not (d in st[8] and ds[0] == eng_name):
                    wait(("c", ds[0]), ds[6])
            if st[3]:
                pi, use = st[7]
                if use > 1:
                    wait(("d", eng_name, pi), 16 * (use - 1))
            inst = st[1](eng)
            if st[5]:
                if inst is None:
                    raise RuntimeError(f"step {sid} ({st[4]}) needs inc, no instruction")
                if st[3]:
                    pi, use = st[7]
                    inst.then_inc(sems[("d", eng_name, pi)], 16)
                else:
                    inst.then_inc(sems[("c", eng_name)], 1)

    def engines(self):
        return sorted({st[0] for st in self.steps}, key=ENGINES.index)


DEBUG = False
STAGE = "full"


def _build(nc):
    p = Prog()
    ctx = []
    stack = contextlib.ExitStack()

    def din(name, shape, dt):
        return nc.dram_tensor(name, list(shape), dt, kind="ExternalInput").ap()

    def dscr(name, shape, dt):
        return nc.dram_tensor(name, list(shape), dt, kind="Internal").ap()

    def sb(name, shape, dt):
        h = nc.sbuf_tensor(name, list(shape), dt)
        ctx.append(h)
        return h.__enter__()

    x_in = din("x", [S, D], F32)
    y_out = nc.dram_tensor("y", [S, D], F32, kind="ExternalOutput").ap()
    idf = din("idf", [128, 128], F32)
    idb = din("idb", [128, 128], BF16)
    w_bw1 = din("bw1", [FF, D], F32)
    w_bw2 = din("bw2", [D, FF], F32)
    w_conv_w = din("conv_w", [FF, 1, 3], F32)
    w_conv_b = din("conv_b", [FF], F32)
    kramp = din("kramp", [1, S], F32)
    gsel = din("gsel", [128, 8], F32)
    w_vw = din("vw", [D, D], F32)
    w_ow = din("ow", [D, D], F32)
    w_cqw = din("cqw", [D, H * CR], F32)
    w_ckw = din("ckw", [D, H * CR], F32)
    w_qmw = din("qmw", [D, H], F32)
    w_kmw = din("kmw", [D, H], F32)
    w_gw = din("gw", [D, H * M], F32)
    w_mw = din("mw", [D, H * RMOD], F32)
    w_basis = din("basis", [RMOD, H, M], F32)
    w_freqs = din("freqs", [H, M, NWV, 2], F32)
    w_amps = din("amps", [H, M, NWV], F32)
    w_phases = din("phases", [H, M, NWV], F32)
    w_gb = din("gb", [H * M], F32)
    w_mb = din("mb", [H * RMOD], F32)
    w_vb = din("vb", [D], F32)
    w_ob = din("ob", [D], F32)
    w_fw1 = din("fw1", [D, FF], F32)
    w_fb1 = din("fb1", [FF], F32)
    w_fw2 = din("fw2", [FF, D], F32)
    w_fb2 = din("fb2", [D], F32)
    w_ln = din("lnv", [4, D], F32)

    PB = []
    for i in range(8):
        h = nc.psum_tensor(f"pb{i}", [128, 512], F32)
        ctx.append(h)
        PB.append(h.__enter__())
    blr = [None] * 8

    XA = sb("XA", [128, NDC, S], F32)
    ARW = sb("ARW", [128, 32768], BF16)
    yqt = sb("yqt", [128, NDC, S], BF16)
    ones_b = sb("ones_b", [128, 1], BF16)
    s_idf = sb("s_idf", [128, 128], F32)
    s_idb = sb("s_idb", [128, 128], BF16)

    wcol = sb("wcol", [128, 8], F32)
    cwsb = sb("cwsb", [128, NFO, 3], F32)
    cbsb = sb("cbsb", [128, NFO], F32)
    sc_f1 = sb("sc_f1", [128, S], F32)
    sc_f2 = sb("sc_f2", [128, S], F32)
    PHU = sb("PHU", [128, 6656], F32)
    repA = PHU[:, 0:S]
    repG = PHU[:, S:2 * S]
    f_rep = PHU[:, 0:S]
    h3buf = PHU[:, S:2 * S]
    ph_h3t = PHU[:, 0:4096].rearrange("p (a b) -> p a b", a=NFO)
    ph_h3q = PHU[:, 4096:6144].bitcast(BF16).rearrange("p (a b) -> p a b", a=NFO)
    ph_g2 = PHU[:, 6144:6272]
    ph_f2 = PHU[:, 6272:6400]
    S1row = PHU[0:1, 2048:4096]
    S2row = PHU[0:1, 4096:6144]

    _stg = yqt[:, 0:2, :].rearrange("p a b -> p (a b)").bitcast(F32)
    xtm = _stg[:, 0:D]
    xtm2 = _stg[:, D:2 * D]

    def yqv(dc):
        return yqt[:, dc, :]

    sc_bf = yqt[:, 0, :]
    sc_bf2 = yqt[:, 1, :]

    if DEBUG:
        d_probe = dscr("scr_probe", [64, 128, 512], F32)
        pr_stg = sb("pr_stg", [128, 512], F32)
        pr_state = [0, None]

        def probe(src_ap, deps):
            slot = pr_state[0]
            pr_state[0] += 1
            nr = src_ap.shape[0]
            ncol = src_ap.shape[-1]
            cp = p.add("vector", (lambda e: e.tensor_copy(pr_stg[0:nr, 0:ncol], src_ap)),
                       list(deps) + [pr_state[1]], tag=f"probe_cp{slot}")
            dm = p.dma("gpsimd", d_probe[slot][0:nr, 0:ncol], pr_stg[0:nr, 0:ncol],
                       deps=[cp], tag=f"probe_dma{slot}")
            pr_state[1] = dm
            return cp
    d_h3 = dscr("scr_h3", [FF, S], BF16)
    d_x1 = dscr("scr_x1", [D, S], F32)
    d_ctx = dscr("scr_ctx", [D, S], BF16)
    d_u = dscr("scr_u", [FF, S], BF16)
    d_stats = dscr("scr_stats", [8, S], F32)
    d_dext = dscr("scr_dext", [H, 2560], F32)
    d_w1t = dscr("scr_w1t", [NFO * NDC * 128, 128], BF16)
    d_w2t = dscr("scr_w2t", [NDC * NFO * 128, 128], BF16)

    ld_idf = p.dma("sync", s_idf[:], idf, tag="ld_idf")
    ld_idb = p.dma("sync", s_idb[:], idb, tag="ld_idb")
    ones_dep = p.add("vector", lambda e: e.memset(ones_b[:], 1.0), tag="ones_b")
    rc_dep = p.add("vector", lambda e: e.memset(wcol[:, 3:4], RC), tag="rc_col")
    p.add("gpsimd", lambda e: e.load_library(_LIB_ATTN), tag="load_lib")

    def mm(bank, lhsT, rhs, start, stop, deps, tag="mm", ncols=None, nrows=128, rowoff=0, coloff=0):
        n = ncols if ncols is not None else rhs.shape[-1]
        dd = list(deps)
        if start and blr[bank] is not None:
            dd.append(blr[bank])
        return p.add("tensor", (lambda e, l=lhsT, r=rhs, s=start, t=stop, b=PB[bank], n=n,
                                nr=nrows, ro=rowoff, co=coloff:
                                e.matmul(b[ro:ro + nr, co:co + n], l, r, start=s, stop=t)), dd, tag=tag)

    def consume(bank, sid):
        blr[bank] = sid

    prev = [None, None]
    for tt in range(16):
        buf = xtm if tt % 2 == 0 else xtm2
        ld = p.dma("sync", buf[:], x_in[tt * 128:(tt + 1) * 128, :],
                   deps=[prev[tt % 2]], tag=f"ldx{tt}")
        for dc in range(NDC):
            bi = dc % 2
            tr = p.add("tensor", (lambda e, b=buf, dc=dc, bi=bi:
                                  e.transpose(PB[bi][:, 0:128], b[:, dc * 128:(dc + 1) * 128], s_idf[:])),
                       [ld, ld_idf, blr[bi]], tag=f"trx{tt}_{dc}")
            cp = p.add("vector", (lambda e, dc=dc, tt=tt, bi=bi:
                                  e.tensor_copy(XA[:, dc, tt * 128:(tt + 1) * 128], PB[bi][:, 0:128])),
                       [tr], tag=f"cpx{tt}_{dc}")
            consume(bi, cp)
            prev[tt % 2] = cp
    x_dep = max(prev)

    ld_cw = p.dma("sync", cwsb[:], bass.AP(w_conv_w.tensor, 0, [[3, 128], [384, NFO], [1, 3]]), tag="ld_cw", allow_slow_non_contiguous=True)
    ld_cb = p.dma("sync", cbsb[:], bass.AP(w_conv_b.tensor, 0, [[1, 128], [128, NFO]]), tag="ld_cb", allow_slow_non_contiguous=True)

    wld = sc_f2[:, 0:FF // 2]
    wtern = PHU[:, S:2 * S].bitcast(BF16)[:, 0:FF // 2]
    wstg = PHU[:, S:2 * S].bitcast(BF16)[:, FF // 2:FF]

    def record_accums(specs):
        states = []
        seq = []
        for si, (tag, w_dram, nrt, sum_col, rd_col, bufs, queues) in enumerate(specs):
            zz = p.add("vector", (lambda e, sum_col=sum_col: e.memset(
                wcol[:, sum_col:sum_col + 1], 0.0)), tag=f"{tag}_z")
            states.append({"last": zz, "prev": [None] * len(bufs), "k": 0})
            seq.append([(si, i, hf) for i in range(nrt) for hf in (0, 1)])
        idx = [0] * len(seq)
        total = sum(len(s) for s in seq)
        for j in range(total):
            si = max(range(len(seq)),
                     key=lambda x: (len(seq[x]) - idx[x]) / len(seq[x]) if idx[x] < len(seq[x]) else -1)
            tag, w_dram, nrt, sum_col, rd_col, bufs, queues = specs[si]
            st = states[si]
            _, i, hf = seq[si][idx[si]]
            idx[si] += 1
            HL = w_dram.shape[1] // 2
            b = st["k"] % len(bufs)
            buf = bufs[b]
            ld = p.dma(queues[st["k"] % len(queues)], buf[:, 0:HL],
                       w_dram[i * 128:(i + 1) * 128, hf * HL:(hf + 1) * HL],
                       deps=[st["prev"][b]], tag=f"{tag}_l{i}_{hf}")
            rd = p.add("vector", (lambda e, buf=buf, rd_col=rd_col, HL=HL: e.tensor_reduce(
                wcol[:, rd_col:rd_col + 1], buf[:, 0:HL], AX.X,
                OP.add, apply_absolute_value=True)),
                [ld], tag=f"{tag}_r{i}_{hf}")
            st["prev"][b] = rd
            st["last"] = p.add("vector", (lambda e, sum_col=sum_col, rd_col=rd_col: e.tensor_add(
                wcol[:, sum_col:sum_col + 1], wcol[:, sum_col:sum_col + 1],
                wcol[:, rd_col:rd_col + 1])),
                [rd, st["last"]], tag=f"{tag}_a{i}_{hf}")
            st["k"] += 1
        return [st["last"] for st in states]

    def prep_bitweight(tag, w_dram, nrt, mw_slot, dram_stash, to_arena,
                       sum_col, acc_last, tern_queue, tern_bufs, tern_gate=None):
        IN = w_dram.shape[1]
        nic = IN // 128
        HL = IN // 2
        nih = nic // 2
        ar = p.add("gpsimd", (lambda e, sum_col=sum_col: e.partition_all_reduce(
            wcol[:, sum_col:sum_col + 1], wcol[:, sum_col:sum_col + 1], 128, ReduceOp.add)),
            [acc_last], tag=f"{tag}_ar")
        nel = float(nrt * 128 * IN)
        mw = p.add("vector", (lambda e, nel=nel, mw_slot=mw_slot, sum_col=sum_col: e.tensor_scalar(
            wcol[:, mw_slot:mw_slot + 1], wcol[:, sum_col:sum_col + 1], 1.0 / nel, 1e-5,
            OP.mult, OP.max)),
            [ar], tag=f"{tag}_mw")
        sr = p.add("vector", (lambda e, mw_slot=mw_slot, sum_col=sum_col: e.reciprocal(
            wcol[:, sum_col:sum_col + 1], wcol[:, mw_slot:mw_slot + 1])), [mw], tag=f"{tag}_sr")
        if DEBUG and tag == "w1p":
            probe(wcol[:, 0:8], [mw])
            probe(wcol[:, 0:8], [sr])
        prev_t1 = [tern_gate] * len(tern_bufs)
        prev_t3 = None
        prev_st = None
        last_tile = None
        kk = 0
        for i in range(nrt):
            for hf in range(2):
                tb = kk % len(tern_bufs)
                buf = tern_bufs[tb]
                kk += 1
                ld = p.dma(tern_queue, buf[:, 0:HL], w_dram[i * 128:(i + 1) * 128, hf * HL:(hf + 1) * HL],
                           deps=[sr, prev_t1[tb]], tag=f"{tag}_m{i}_{hf}")
                t1 = p.add("scalar", (lambda e, sum_col=sum_col, buf=buf: e.activation(
                    sc_f1[:, 0:HL], buf[:, 0:HL], AF.Identity,
                    scale=wcol[:, sum_col:sum_col + 1], bias=wcol[:, 3:4])),
                    [ld, rc_dep, sr, prev_t3], tag=f"{tag}_t1_{i}_{hf}")
                prev_t1[tb] = t1
                if DEBUG and tag == "w1p" and i == 0 and hf == 0:
                    prev_t1[tb] = probe(buf[:, 0:512], [ld, t1])
                    probe(sc_f1[:, 0:512], [t1])
                t2 = p.add("vector", (lambda e: e.tensor_scalar(sc_f1[:, 0:HL], sc_f1[:, 0:HL],
                                                                RC, 1.0, OP.subtract, OP.min)),
                           [t1], tag=f"{tag}_t2_{i}_{hf}")
                t3 = p.add("vector", (lambda e: e.tensor_scalar(wtern[:, 0:HL], sc_f1[:, 0:HL],
                                                                -1.0, None, OP.max)),
                           [t2], tag=f"{tag}_t3_{i}_{hf}")
                prev_t3 = t3
                if DEBUG and tag == "w1p" and i == 0 and hf == 0:
                    probe(sc_f1[:, 0:512], [t2])
                    probe(wtern.bitcast(F32)[:, 0:256], [t3])
                evs = []
                for j in range(nih):
                    ic = hf * nih + j
                    bi = 2 + j % 2
                    tr = p.add("tensor", (lambda e, j=j, bi=bi:
                                          e.transpose(PB[bi].bitcast(BF16)[:, 0:128],
                                                      wtern[:, j * 128:(j + 1) * 128], s_idb[:])),
                               [t3, ld_idb, blr[bi]], tag=f"{tag}_tr{i}_{ic}")
                    if to_arena:
                        cp = p.add("vector", (lambda e, k=i * nic + ic, bi=bi:
                                              e.tensor_copy(ARW[:, k * 128:(k + 1) * 128],
                                                            PB[bi].bitcast(BF16)[:, 0:128])),
                                   [tr], tag=f"{tag}_cp{i}_{ic}")
                    else:
                        cp = p.add("vector", (lambda e, j=j, bi=bi:
                                              e.tensor_copy(wstg[:, j * 128:(j + 1) * 128],
                                                            PB[bi].bitcast(BF16)[:, 0:128])),
                                   [tr, prev_st], tag=f"{tag}_cp{i}_{ic}")
                    consume(bi, cp)
                    evs.append(cp)
                src_ap = ARW[:, (i * nic + hf * nih) * 128:(i * nic + (hf + 1) * nih) * 128]                     if to_arena else wstg[:, 0:HL]
                dst_ap = bass.AP(dram_stash.tensor, (i * nic + hf * nih) * 128 * 128,
                                 [[128, 128], [128 * 128, nih], [1, 128]])
                st = p.dma("gpsimd", dst_ap, src_ap, deps=evs, tag=f"{tag}_st{i}_{hf}")
                prev_st = st
                last_tile = st
        return last_tile, mw

    w2acc = [yqt[:, 4:6, :].rearrange("p a b -> p (a b)").bitcast(F32),
             yqt[:, 6:8, :].rearrange("p a b -> p (a b)").bitcast(F32)]
    acc1, acc2 = record_accums([
        ("w1p", w_bw1, NFO, 1, 2, [sc_f2[:, 0:512], sc_f2[:, 512:1024]],
         ("scalar",)),
        ("w2p", w_bw2, NDC, 6, 7, w2acc, ("gpsimd",)),
    ])
    w1_rdy, w1_mw = prep_bitweight("w1p", w_bw1, NFO, 4, d_w1t, to_arena=True,
                                   sum_col=1, acc_last=acc1, tern_queue="scalar",
                                   tern_bufs=[sc_f2[:, 0:512], sc_f2[:, 512:1024]])
    w2_rdy, w2_mw = prep_bitweight("w2p", w_bw2, NDC, 5, d_w2t, to_arena=False,
                                   sum_col=6, acc_last=acc2, tern_queue="gpsimd",
                                   tern_bufs=[sc_f2[:, 0:2048], w2acc[0]],
                                   tern_gate=w1_rdy)

    def emit_ln_stats(tag, xv, deps, eps=1e-5, phu_guard=None):
        mm_last = []
        prev_c = [None, None]
        for dc in range(NDC):
            c1 = p.add("vector", (lambda e, dc=dc: e.tensor_copy(sc_bf, xv(dc))),
                       deps + [prev_c[0]], tag=f"{tag}_cb{dc}")
            c2 = p.add("scalar", (lambda e, dc=dc: e.activation(sc_bf2, xv(dc), AF.Square)),
                       deps + [prev_c[1]], tag=f"{tag}_sq{dc}")
            s_last = []
            for tt in range(4):
                b0, b1 = tt, 4 + tt
                m1 = mm(b0, ones_b[:], sc_bf[:, tt * 512:(tt + 1) * 512],
                        start=(dc == 0), stop=(dc == NDC - 1), deps=[c1, ones_dep],
                        tag=f"{tag}_s1_{dc}_{tt}", nrows=1)
                m2 = mm(b1, ones_b[:], sc_bf2[:, tt * 512:(tt + 1) * 512],
                        start=(dc == 0), stop=(dc == NDC - 1), deps=[c2, ones_dep],
                        tag=f"{tag}_s2_{dc}_{tt}", nrows=1)
                s_last.append((tt, m1, m2))
            prev_c = [s_last[-1][1], s_last[-1][2]]
            if dc == NDC - 1:
                mm_last = s_last
        ev1, ev2 = [], []
        for tt, m1, m2 in mm_last:
            b0, b1 = tt, 4 + tt
            e1 = p.add("vector", (lambda e, tt=tt, b0=b0: e.tensor_copy(
                S1row[:, tt * 512:(tt + 1) * 512], PB[b0][0:1, 0:512])), [m1, phu_guard],
                tag=f"{tag}_e1{tt}")
            consume(b0, e1)
            e2 = p.add("vector", (lambda e, tt=tt, b1=b1: e.tensor_copy(
                S2row[:, tt * 512:(tt + 1) * 512], PB[b1][0:1, 0:512])), [m2], tag=f"{tag}_e2{tt}")
            consume(b1, e2)
            ev1.append(e1)
            ev2.append(e2)
        p1 = p.add("gpsimd", lambda e: e.partition_broadcast(repA[:, :], S1row, 128),
                   ev1, tag=f"{tag}_pb1")
        p2 = p.add("gpsimd", lambda e: e.partition_broadcast(repG[:, :], S2row, 128),
                   ev2 + [p1], tag=f"{tag}_pb2")
        a1 = p.add("vector", (lambda e: e.tensor_scalar(repA[:, :], repA[:, :], 1.0 / D, None, OP.mult)),
                   [p1], tag=f"{tag}_mu")
        a2 = p.add("vector", lambda e: e.tensor_mul(sc_f1[:], repA[:, :], repA[:, :]),
                   [a1], tag=f"{tag}_mu2")
        a3 = p.add("vector", (lambda e: e.scalar_tensor_tensor(repG[:, :], repG[:, :], 1.0 / D,
                                                               sc_f1[:], OP.mult, OP.subtract)),
                   [a2, p2], tag=f"{tag}_var")
        a4 = p.add("vector", (lambda e, eps=eps: e.tensor_scalar(repG[:, :], repG[:, :], eps, None, OP.add)),
                   [a3], tag=f"{tag}_veps")
        a5 = p.add("vector", lambda e: e.reciprocal(repG[:, :], repG[:, :]), [a4], tag=f"{tag}_rcp")
        a6 = p.add("scalar", lambda e: e.activation(repG[:, :], repG[:, :], AF.Sqrt), [a5], tag=f"{tag}_rst")
        return a1, a6

    def emit_quant(tag, xv, deps, mu_dep, inv_dep, mw_slot, extra_f, guard=None):
        mnbuf = sc_f1[:, 0:S]
        za = p.add("vector", lambda e: e.memset(sc_f2[:], -3.0e38), deps + [guard], tag=f"{tag}_z")
        zb = p.add("vector", lambda e: e.memset(mnbuf, 3.0e38), deps + [guard], tag=f"{tag}_z2")
        last = zb
        for dc in range(NDC):
            t1 = p.add("vector", (lambda e, dc=dc: e.tensor_tensor(sc_f2[:], sc_f2[:], xv(dc), OP.max)),
                       deps + [za, last], tag=f"{tag}_mx{dc}")
            t2 = p.add("vector", (lambda e, dc=dc: e.tensor_tensor(mnbuf, mnbuf, xv(dc), OP.min)),
                       [t1], tag=f"{tag}_mn{dc}")
            last = t2
        ng = p.add("vector", (lambda e: e.tensor_scalar(mnbuf, mnbuf, -1.0, None, OP.mult)),
                   [last], tag=f"{tag}_ng")
        ar1 = p.add("gpsimd", lambda e: e.partition_all_reduce(sc_f2[:], sc_f2[:], 128, ReduceOp.max),
                    [last], tag=f"{tag}_ar1")
        ar2 = p.add("gpsimd", lambda e: e.partition_all_reduce(mnbuf, mnbuf, 128, ReduceOp.max),
                    [ng, ar1], tag=f"{tag}_ar2")
        c1 = p.add("vector", (lambda e: e.tensor_sub(sc_f2[:], sc_f2[:], repA[:, :])),
                   [ar1, mu_dep], tag=f"{tag}_c1")
        c2 = p.add("vector", (lambda e: e.tensor_add(sc_f1[:], mnbuf, repA[:, :])),
                   [ar2], tag=f"{tag}_c2")
        ar = p.add("vector", (lambda e: e.tensor_tensor(sc_f2[:], sc_f2[:], sc_f1[:], OP.max)),
                   [c1, c2], tag=f"{tag}_m")
        r1 = p.add("vector", (lambda e: e.tensor_tensor(sc_f2[:], sc_f2[:], repG[:, :], OP.mult)),
                   [ar, inv_dep], tag=f"{tag}_my")
        r2 = p.add("vector", (lambda e: e.tensor_scalar(sc_f2[:], sc_f2[:], 1e-5, None, OP.max)),
                   [r1], tag=f"{tag}_cl")
        r3 = p.add("vector", lambda e: e.reciprocal(sc_f1[:], sc_f2[:]), [r2], tag=f"{tag}_rc")
        r4 = p.add("vector", (lambda e: e.tensor_scalar(sc_f1[:], sc_f1[:], 127.0, None, OP.mult)),
                   [r3], tag=f"{tag}_127")
        r5 = p.add("vector", lambda e: e.tensor_tensor(repG[:, :], repG[:, :], sc_f1[:], OP.mult),
                   [r4], tag=f"{tag}_g")
        r6 = p.add("vector", lambda e: e.tensor_tensor(repA[:, :], repA[:, :], repG[:, :], OP.mult),
                   [r5], tag=f"{tag}_mg")
        qd = []
        for dc in range(NDC):
            q1 = p.add("vector", (lambda e, dc=dc: e.tensor_mul(sc_f1[:], xv(dc), repG[:, :])),
                       deps + [r6], tag=f"{tag}_q1{dc}")
            q2 = p.add("vector", (lambda e: e.tensor_sub(sc_f1[:], sc_f1[:], repA[:, :])),
                       [q1], tag=f"{tag}_q2{dc}")
            q3 = p.add("vector", (lambda e, dc=dc: e.tensor_scalar(yqv(dc), sc_f1[:],
                                                                   RC, RC, OP.add, OP.subtract)),
                       [q2], tag=f"{tag}_q3{dc}")
            qd.append(q3)
        rf = p.add("vector", (lambda e, mw_slot=mw_slot, extra_f=extra_f: e.tensor_scalar(
            f_rep[:, :], sc_f2[:], wcol[:, mw_slot:mw_slot + 1], extra_f / 127.0,
            OP.mult, OP.mult)), [r2] + qd, tag=f"{tag}_f")
        return qd, [rf]

    def emit_bitffn_sweep1(tag, w1_ready, qd, frep):
        sts = []
        prev_sil = None
        prev_st = None
        prev_b1 = None
        for fo in range(NFO):
            grp = (fo % 2) * 4
            if DEBUG and tag == "A" and fo in (0, 31):
                probe(ARW.bitcast(F32)[:, 15872:16384], [prev_st, w1_ready] + qd)
            mml = []
            for tt in range(4):
                bi = grp + tt
                sid = None
                for ic in range(NDC):
                    sid = mm(bi, ARW[:, (fo * NDC + ic) * 128:(fo * NDC + ic) * 128 + 128],
                             yqt[:, ic, tt * 512:(tt + 1) * 512],
                             start=(ic == 0), stop=(ic == NDC - 1),
                             deps=(qd + [w1_ready] if ic == 0 else []), tag=f"{tag}1mm{fo}_{tt}_{ic}")
                mml.append(sid)
            evs = []
            if DEBUG and tag == "A" and fo == 31:
                probe(PB[grp][:, 0:512], [mml[0]])
                probe(f_rep[:, 0:512], frep + [mml[0]])
            for tt in range(4):
                bi = grp + tt
                sid = p.add("vector", (lambda e, tt=tt, bi=bi: e.tensor_mul(
                    h3buf[:, tt * 512:(tt + 1) * 512], PB[bi][:, 0:512], f_rep[:, tt * 512:(tt + 1) * 512])),
                    [mml[tt], prev_st] + frep, tag=f"{tag}1ev{fo}_{tt}")
                consume(bi, sid)
                evs.append(sid)
            if DEBUG and tag == "A" and fo in (0, 31):
                probe(h3buf[:, 0:512], evs)
            a1 = p.add("vector", (lambda e, fo=fo: e.tensor_scalar(
                sc_f1[:], h3buf[:], cwsb[:, fo, 1:2], cbsb[:, fo:fo + 1], OP.mult, OP.add)),
                evs + [ld_cw, ld_cb, prev_sil, prev_b1], tag=f"{tag}cva{fo}")
            b1 = p.add("vector", (lambda e, fo=fo: e.scalar_tensor_tensor(
                sc_f1[:, 1:S], h3buf[:, 0:S - 1], cwsb[:, fo, 0:1], sc_f1[:, 1:S], OP.mult, OP.add)),
                [a1], tag=f"{tag}cvb{fo}")
            prev_b1 = b1
            c1 = p.add("vector", (lambda e, fo=fo: e.scalar_tensor_tensor(
                sc_f1[:, 0:S - 1], h3buf[:, 1:S], cwsb[:, fo, 2:3], sc_f1[:, 0:S - 1], OP.mult, OP.add)),
                [b1], tag=f"{tag}cvc{fo}")
            if DEBUG and tag == "A" and fo in (0, 31):
                probe(sc_f1[:, 0:512], [c1])
            s1 = p.add("scalar", (lambda e: e.activation(h3buf[:], sc_f1[:], AF.Silu)),
                       [c1], tag=f"{tag}si{fo}")
            prev_sil = s1
            if DEBUG and tag == "A" and fo in (0, 31):
                probe(h3buf[:, 0:512], [s1])
            st = p.dma("gpsimd", d_h3[fo * 128:(fo + 1) * 128, :], h3buf[:], deps=[s1], tag=f"{tag}1st{fo}")
            prev_st = st
            sts.append(st)
        return sts

    def load_arena_from_stash(tag, stash, ntiles, deps):
        out = []
        per = ntiles // 8
        for i in range(8):
            src = bass.AP(stash.tensor, i * per * 128 * 128,
                          [[128, 128], [128 * 128, per], [1, 128]])
            out.append(p.dma("sync", ARW[:, i * per * 128:(i + 1) * per * 128], src,
                             deps=deps, tag=f"{tag}_{i}"))
        return out

    def emit_bitffn_sweep2(tag, w2_ready, mw2_dep, mw2_slot):
        upd = []
        prev_mmlast = None
        prev_upd = [None] * 8
        T = 512
        h3t = yqt[:, :, :].rearrange("p a b -> p (a b)").rearrange(
            "p (a b) -> p a b", a=NFO)
        g2rep = PHU[:, 0:512]
        f2rep = PHU[:, 512:1024]
        for ti in range(4):
            t0 = ti * T
            sap = bass.AP(d_h3.tensor, t0, [[S, 128], [128 * S, NFO], [1, T]])
            ld = p.dma("sync", h3t[:, :, :], sap, deps=[prev_mmlast], tag=f"{tag}ld{ti}")
            rd = p.add("vector", (lambda e: e.tensor_reduce(
                sc_f2[:, 0:T], h3t[:, :, :].transpose([0, 2, 1]), AX.X, OP.max,
                apply_absolute_value=True)), [ld], tag=f"{tag}rd{ti}")
            ar = p.add("gpsimd", (lambda e: e.partition_all_reduce(sc_f2[:, 0:T], sc_f2[:, 0:T],
                                                                   128, ReduceOp.max)),
                       [rd], tag=f"{tag}ar{ti}")
            c1 = p.add("vector", (lambda e: e.tensor_scalar(sc_f2[:, 0:T], sc_f2[:, 0:T],
                                                            1e-5, None, OP.max)), [ar], tag=f"{tag}cl{ti}")
            f2 = p.add("vector", (lambda e, mw2_slot=mw2_slot: e.tensor_scalar(
                f2rep[:], sc_f2[:, 0:T], wcol[:, mw2_slot:mw2_slot + 1], 0.5 / 127.0,
                OP.mult, OP.mult)), [c1, mw2_dep], tag=f"{tag}f{ti}")
            g1 = p.add("vector", lambda e: e.reciprocal(g2rep[:], sc_f2[:, 0:T]), [c1], tag=f"{tag}rc{ti}")
            g2 = p.add("vector", (lambda e: e.tensor_scalar(g2rep[:], g2rep[:], 127.0, None, OP.mult)),
                       [g1], tag=f"{tag}g{ti}")
            g2b = g2rep.unsqueeze(1).broadcast_to((128, NFO, T))
            q1 = p.add("vector", (lambda e, g2b=g2b: e.tensor_tensor(
                h3t[:, :, :], h3t[:, :, :], g2b, OP.mult)), [g2, ld], tag=f"{tag}q1_{ti}")
            q3 = p.add("vector", (lambda e: e.tensor_scalar(
                h3t[:, :, :], h3t[:, :, :], RC, RC, OP.add, OP.subtract)),
                [q1], tag=f"{tag}q3_{ti}")
            for do in range(NDC):
                sid = None
                for kc in range(NFO):
                    sid = mm(do, ARW[:, (do * NFO + kc) * 128:(do * NFO + kc) * 128 + 128],
                             h3t[:, kc, :], start=(kc == 0), stop=(kc == NFO - 1),
                             deps=([q3] + w2_ready if kc == 0 else []), ncols=T,
                             tag=f"{tag}2mm{ti}_{do}_{kc}")
                if do == NDC - 1:
                    prev_mmlast = sid
                t1 = p.add("vector", (lambda e, do=do: e.tensor_mul(sc_f1[:, 0:T], PB[do][:, 0:T],
                                                                    f2rep[:])),
                           [sid, f2, prev_upd[do]], tag=f"{tag}ev{ti}_{do}")
                consume(do, t1)
                u1 = p.add("vector", (lambda e, do=do, t0=t0: e.tensor_add(
                    XA[:, do, t0:t0 + T], XA[:, do, t0:t0 + T], sc_f1[:, 0:T])),
                    [t1], tag=f"{tag}up{ti}_{do}")
                prev_upd[do] = u1
                upd.append(u1)
        return upd

    def emit_bitffn(tag, w1_ready, w2_ready_stash, mw2_dep, x_deps):
        muD, invD = emit_ln_stats(tag + "ln", lambda dc: XA[:, dc, :], x_deps,
                                  phu_guard=w2_rdy)
        qd, frep = emit_quant(tag + "q", lambda dc: XA[:, dc, :], x_deps, muD, invD, 4, 1.0, guard=w2_rdy)
        if DEBUG and tag == "A":
            probe(yqv(0)[:, 0:512], qd)
            probe(f_rep[:, 0:512], frep)
            probe(repG[:, 0:512], qd)
        sts = emit_bitffn_sweep1(tag, w1_ready, qd, frep)
        p.barrier()
        w2ld = load_arena_from_stash(tag + "w2", d_w2t, NDC * NFO, [w2_ready_stash])
        upd = emit_bitffn_sweep2(tag, w2ld, mw2_dep, 5)
        p.barrier()
        return upd


    NBLK = 6

    def hb(h):
        return (32 * (h % 3), h // 3)

    def emit_attention(x_deps):
        rhsQ = XA[:, 0:3, :].rearrange("p a b -> p (a b)").bitcast(BF16) \
                 .rearrange("p (a b) -> p a b", a=NBLK)
        lhsK = XA[:, 3:6, :].rearrange("p a b -> p (a b)").bitcast(BF16) \
                 .rearrange("p (a b) -> p a b", a=NBLK)
        vfm = ARW[:, 0:16384]
        Pt = ARW[:, 16384:24576]
        vw_a = ARW[:, 16384:24576]
        ow_a = ARW[:, 16384:24576]
        cq_a = ARW[:, 24576:25600]
        ck_a = ARW[:, 25600:26624]
        qmw_a = ARW[:, 26624:26752]
        kmw_a = ARW[:, 26752:26880]
        vaug = [ARW[:, 26880 + r * 1040:26880 + (r + 1) * 1040] for r in range(2)]
        ctxstg = ARW[0:64, 29056:29568]
        acol = PHU[:, 6400:6656]
        a_vb = acol[:, 0:8]
        a_ob = acol[:, 8:16]
        a_fb1 = acol[:, 16:48]
        a_fb2 = acol[:, 48:56]
        a_ln = acol[:, 56:88]
        a_gb = acol[:, 88:90]
        a_mb = acol[:, 90:91]
        a_pool = acol[:, 91:99]
        a_modc = acol[:, 99:100]
        a_fq = acol[:, 100:132]
        a_ph = acol[:, 132:148]
        a_am = acol[:, 148:164]
        a_wts = acol[:, 164:166]
        a_gsel = acol[:, 166:174]
        gw_a = PHU[:, 4096:6144]
        mw_a = PHU[:, 2048:3072]
        bas_a = PHU[0:8, 3072:3328]
        glsb = PHU[:, 3328:3330]
        G16 = PHU[0:16, 3332:3348]
        G16b = PHU[0:16, 3348:3356]
        modT = PHU[0:8, 3364:3380]
        GB = [PHU[:, 0:2432], PHU[:, 2432:4864]]
        rrow = PHU[0:1, 4864:5376]
        rrep = PHU[0:64, 5376:5888]

        muZ, invZ = emit_ln_stats("zln", lambda dc: XA[:, dc, :], x_deps)
        sv1 = p.dma("gpsimd", d_stats[0:1, :], repA[0:1, :], deps=[muZ, invZ], tag="sv_mu")
        sv2 = p.dma("gpsimd", d_stats[1:2, :], repG[0:1, :], deps=[invZ], tag="sv_inv")
        zd = []
        for dc in range(NDC):
            z1 = p.add("vector", (lambda e, dc=dc: e.tensor_sub(sc_f1[:], XA[:, dc, :], repA[:, :])),
                       x_deps + [muZ, invZ, sv1], tag=f"z1_{dc}")
            z2 = p.add("vector", (lambda e, dc=dc: e.tensor_mul(yqv(dc), sc_f1[:], repG[:, :])),
                       [z1, sv2], tag=f"z2_{dc}")
            zd.append(z2)
        spl = [p.dma("gpsimd", d_x1[dc * 128:(dc + 1) * 128, :], XA[:, dc, :],
                     deps=[zd[-1]], tag=f"spl{dc}") for dc in range(NDC)]

        krrow = sc_f2[0:1, :]
        ld_kr = p.dma("sync", krrow, kramp, deps=x_deps, tag="ldkr")
        prow = sc_f1[0:1, :]
        r1 = p.add("vector", (lambda e: e.tensor_scalar(prow, krrow, 1.0 / S, None, OP.mult)),
                   [ld_kr], tag="p1row")
        sv3 = p.dma("gpsimd", d_stats[4:5, :], prow, deps=[r1], tag="sv_p1")
        r2 = p.add("vector", (lambda e: e.tensor_scalar(prow, krrow, 1e-6, None, OP.add)),
                   [sv3], tag="p2a")
        r3 = p.add("scalar", (lambda e: e.activation(prow, prow, AF.Sqrt)), [r2], tag="p2b")
        r4 = p.add("vector", (lambda e: e.tensor_scalar(prow, prow, float(1.0 / np.sqrt(S)), None, OP.mult)),
                   [r3], tag="p2c")
        r5 = p.add("vector", (lambda e: e.memset(prow[:, 0:1], 0.0)), [r4], tag="p2z")
        sv4 = p.dma("gpsimd", d_stats[5:6, :], prow, deps=[r5], tag="sv_p2")
        r6 = p.add("scalar", (lambda e: e.activation(prow, krrow, AF.Square)), [sv4], tag="w3a")
        r7 = p.add("scalar", (lambda e: e.activation(prow, prow, AF.Exp, scale=float(-1.0 / 524288.0))),
                   [r6], tag="w3b")
        r8 = p.add("vector", (lambda e: e.tensor_scalar(prow, prow, 3.0, None, OP.mult)), [r7], tag="w3c")
        sv5 = p.dma("gpsimd", d_stats[6:7, :], prow, deps=[r8], tag="sv_w3")

        ld_vw = [p.dma("gpsimd", vw_a[:, dc * 1024:(dc + 1) * 1024],
                       w_vw[dc * 128:(dc + 1) * 128, :], deps=x_deps, tag=f"ldvw{dc}") for dc in range(NDC)]
        ld_cq = [p.dma("gpsimd", cq_a[:, dc * 128:(dc + 1) * 128],
                       w_cqw[dc * 128:(dc + 1) * 128, :], deps=x_deps, tag=f"ldcq{dc}") for dc in range(NDC)]
        ld_ck = [p.dma("gpsimd", ck_a[:, dc * 128:(dc + 1) * 128],
                       w_ckw[dc * 128:(dc + 1) * 128, :], deps=x_deps, tag=f"ldck{dc}") for dc in range(NDC)]
        ld_qm = [p.dma("gpsimd", qmw_a[:, dc * 16:(dc + 1) * 16],
                       w_qmw[dc * 128:(dc + 1) * 128, :], deps=x_deps, tag=f"ldqm{dc}") for dc in range(NDC)]
        ld_km = [p.dma("gpsimd", kmw_a[:, dc * 16:(dc + 1) * 16],
                       w_kmw[dc * 128:(dc + 1) * 128, :], deps=x_deps, tag=f"ldkm{dc}") for dc in range(NDC)]
        def colload(name, dst, wap, n):
            return p.dma("sync", dst, bass.AP(wap.tensor, 0, [[1, 128], [128, n]]),
                         tag=name, allow_slow_non_contiguous=True)
        ld_vb = colload("ldvb", a_vb, w_vb, 8)
        ld_ob = colload("ldob", a_ob, w_ob, 8)
        ld_fb1 = colload("ldfb1", a_fb1, w_fb1, 32)
        ld_fb2 = colload("ldfb2", a_fb2, w_fb2, 8)
        ld_gb = colload("ldgb", a_gb, w_gb, 2)
        ld_mb = colload("ldmb", a_mb, w_mb, 1)
        ld_ln = p.dma("sync", a_ln, bass.AP(w_ln.tensor, 0, [[1, 128], [1024, 4], [128, 8]]),
                      tag="ldln", allow_slow_non_contiguous=True)
        ld_fq = p.dma("sync", a_fq, bass.AP(w_freqs.tensor, 0, [[16, 128], [2048, 2], [1, 16]]),
                      tag="ldfq", allow_slow_non_contiguous=True)
        ld_ph = p.dma("sync", a_ph, bass.AP(w_phases.tensor, 0, [[8, 128], [1024, 2], [1, 8]]),
                      tag="ldph", allow_slow_non_contiguous=True)
        ld_am = p.dma("sync", a_am, bass.AP(w_amps.tensor, 0, [[8, 128], [1024, 2], [1, 8]]),
                      tag="ldam", allow_slow_non_contiguous=True)
        ld_gs = p.dma("sync", a_gsel, gsel, tag="ldgs")
        ld_bas = p.dma("sync", bas_a, bass.AP(w_basis.tensor, 0, [[256, 8], [1, 256]]),
                       deps=zd + [muZ, invZ], tag="ldbas")
        ld_gw = [p.dma("sync", gw_a[:, dc * 256:(dc + 1) * 256],
                       w_gw[dc * 128:(dc + 1) * 128, :], deps=zd + [sv5, muZ, invZ], tag=f"ldgw{dc}")
                 for dc in range(NDC)]
        ld_mw = [p.dma("sync", mw_a[:, dc * 128:(dc + 1) * 128],
                       w_mw[dc * 128:(dc + 1) * 128, :], deps=zd + [muZ, invZ], tag=f"ldmw{dc}")
                 for dc in range(NDC)]
        c2pi = float(1.0 / (2 * np.pi))
        fqp = p.add("vector", (lambda e: e.tensor_scalar(a_fq, a_fq, c2pi, None, OP.mult)),
                    [ld_fq], tag="fqp")
        php = p.add("vector", (lambda e: e.tensor_scalar(a_ph, a_ph, c2pi, None, OP.mult)),
                    [ld_ph], tag="php")

        qct = sc_f2.bitcast(BF16)[:, 0:2048]
        kct = sc_f2.bitcast(BF16)[:, 2048:4096]
        prj = []
        for which in ("qc", "kc"):
            wa = cq_a if which == "qc" else ck_a
            lw = ld_cq if which == "qc" else ld_ck
            for tt in range(4):
                sid = None
                for dc in range(NDC):
                    sid = mm(4 + tt % 2, wa[:, dc * 128:(dc + 1) * 128],
                             yqv(dc)[:, tt * 512:(tt + 1) * 512], start=(dc == 0), stop=(dc == NDC - 1),
                             deps=(zd + lw if dc == 0 else []), tag=f"{which}{tt}_{dc}")
                if which == "qc":
                    ev = p.add("vector", (lambda e, tt=tt, b=4 + tt % 2: e.tensor_scalar(
                        qct[:, tt * 512:(tt + 1) * 512], PB[b][:, 0:512], MIXSC, None, OP.mult)),
                        [sid], tag=f"qcev{tt}")
                else:
                    ev = p.add("vector", (lambda e, tt=tt, b=4 + tt % 2: e.tensor_copy(
                        kct[:, (3 - tt) * 512:(4 - tt) * 512], PB[b][:, 0:512][:, ::-1])),
                        [sid], tag=f"kcev{tt}")
                consume(4 + tt % 2, ev)
                prj.append(ev)
        rq_deps, lk_deps = [], []
        for h in range(H):
            base, blk = hb(h)
            rq_deps.append(p.dma("gpsimd", rhsQ[base:base + 8, blk, :], qct[8 * h:8 * h + 8, :],
                                 deps=prj + spl, tag=f"rpq{h}"))
            lk_deps.append(p.dma("gpsimd", lhsK[base:base + 8, blk, :], kct[8 * h:8 * h + 8, :],
                                 deps=prj + spl, tag=f"rpk{h}"))
        qmf = sc_f1[0:16, :]
        hi16 = sc_f2.bitcast(BF16)[0:16, 0:2048]
        lo16 = sc_f2.bitcast(BF16)[0:16, 2048:4096]
        for which, wap, lw, outrow, dst in (("qm", qmw_a, ld_qm, 8, rhsQ), ("km", kmw_a, ld_km, 10, lhsK)):
            evs = []
            for tt in range(4):
                sid = None
                for dc in range(NDC):
                    sid = mm(6, wap[:, dc * 16:(dc + 1) * 16],
                             yqv(dc)[:, tt * 512:(tt + 1) * 512], start=(dc == 0), stop=(dc == NDC - 1),
                             deps=(zd + lw if dc == 0 else []), nrows=16, tag=f"{which}{tt}_{dc}")
                ev = p.add("vector", (lambda e, tt=tt: e.tensor_copy(
                    qmf[:, tt * 512:(tt + 1) * 512], PB[6][0:16, 0:512])), [sid, sv5], tag=f"{which}ev{tt}")
                consume(6, ev)
                evs.append(ev)
            guards = list(rq_deps) + list(lk_deps)
            if which == "qm":
                h1s = p.add("vector", (lambda e: e.tensor_copy(hi16, qmf)), evs + guards, tag="qmhi")
                l1s = p.add("vector", (lambda e: e.tensor_tensor(lo16, qmf, hi16, OP.subtract)),
                            [h1s], tag="qmlo")
            else:
                h1s = p.add("vector", (lambda e: e.tensor_copy(hi16, qmf[:, ::-1])), evs + guards, tag="kmhi")
                l1s = p.add("vector", (lambda e: e.tensor_tensor(lo16, qmf[:, ::-1], hi16, OP.subtract)),
                            [h1s], tag="kmlo")
            for h in range(H):
                base, blk = hb(h)
                rq_deps.append(p.dma("gpsimd", dst[base + outrow:base + outrow + 1, blk, :],
                                     hi16[h:h + 1, :], deps=[h1s, l1s], tag=f"{which}h{h}"))
                rq_deps.append(p.dma("gpsimd", dst[base + outrow + 1:base + outrow + 2, blk, :],
                                     lo16[h:h + 1, :], deps=[l1s], tag=f"{which}l{h}"))
        ones_row = p.add("vector", lambda e: e.memset(qct[0:1, :], 1.0),
                         rq_deps + lk_deps, tag="ones_row")
        for h in range(H):
            base, blk = hb(h)
            for r in (10, 11):
                rq_deps.append(p.dma("gpsimd", rhsQ[base + r:base + r + 1, blk, :], qct[0:1, :],
                                     deps=[ones_row], tag=f"oq{h}_{r}"))
            for r in (8, 9):
                lk_deps.append(p.dma("gpsimd", lhsK[base + r:base + r + 1, blk, :], qct[0:1, :],
                                     deps=[ones_row], tag=f"ok{h}_{r}"))
        qk_ready = rq_deps + lk_deps
        vfm_deps = []
        for pc in range(8):
            for tt in range(4):
                sid = None
                for dc in range(NDC):
                    sid = mm(4 + tt % 2, vw_a[:, dc * 1024 + pc * 128:dc * 1024 + (pc + 1) * 128],
                             yqv(dc)[:, tt * 512:(tt + 1) * 512], start=(dc == 0), stop=(dc == NDC - 1),
                             deps=(zd + ld_vw if dc == 0 else []), tag=f"v{pc}_{tt}_{dc}")
                ev = p.add("vector", (lambda e, pc=pc, tt=tt, b=4 + tt % 2: e.tensor_scalar(
                    vfm[:, pc * 2048 + (3 - tt) * 512:pc * 2048 + (4 - tt) * 512],
                    PB[b][:, 0:512][:, ::-1], a_vb[:, pc:pc + 1], None, OP.add)),
                    [sid, ld_vb], tag=f"vev{pc}_{tt}")
                consume(4 + tt % 2, ev)
                vfm_deps.append(ev)

        pool_deps = [p.add("vector", (lambda e, dc=dc: e.tensor_reduce(
            a_pool[:, dc:dc + 1], yqv(dc), AX.X, OP.add)), zd, tag=f"pool{dc}")
            for dc in range(NDC)]
        gl_evs = []
        for half in range(2):
            sid = None
            for dc in range(NDC):
                sid = mm(7, gw_a[:, dc * 256 + half * 128:dc * 256 + (half + 1) * 128],
                         a_pool[:, dc:dc + 1], start=(dc == 0), stop=(dc == NDC - 1),
                         deps=(pool_deps + ld_gw if dc == 0 else []), tag=f"gl{half}_{dc}")
            ev = p.add("vector", (lambda e, half=half: e.tensor_scalar(
                glsb[:, half:half + 1], PB[7][:, 0:1], 1.0 / S, a_gb[:, half:half + 1],
                OP.mult, OP.add)), [sid, ld_gb], tag=f"glev{half}")
            consume(7, ev)
            gl_evs.append(ev)
        sid = None
        for dc in range(NDC):
            sid = mm(7, mw_a[:, dc * 128:(dc + 1) * 128], a_pool[:, dc:dc + 1],
                     start=(dc == 0), stop=(dc == NDC - 1),
                     deps=(pool_deps + ld_mw + gl_evs if dc == 0 else []), tag=f"mod{dc}")
        mev = p.add("vector", (lambda e: e.tensor_scalar(a_modc, PB[7][:, 0:1], 1.0 / S,
                                                         a_mb, OP.mult, OP.add)),
                    [sid, ld_mb], tag="modev")
        consume(7, mev)
        ms1 = p.dma("gpsimd", d_stats[7:8, 0:128], a_modc, deps=[mev], tag="mod_out")
        ms2 = p.dma("gpsimd", modT, bass.AP(d_stats.tensor, 7 * S, [[1, 8], [8, 16]]),
                    deps=[ms1], tag="mod_in", allow_slow_non_contiguous=True)
        gr1 = p.dma("gpsimd", G16[0:8, :], glsb[:, 0:1], deps=gl_evs, tag="glrp0")
        gr2 = p.dma("gpsimd", G16[8:16, :], glsb[:, 1:2], deps=gl_evs, tag="glrp1")
        bsid = []
        for h in range(H):
            bsid.append(mm(6, bas_a[0:8, h * 16:(h + 1) * 16], modT[0:8, h:h + 1],
                           start=True, stop=True, deps=[ms2, ld_bas], nrows=16, coloff=h,
                           tag=f"bas{h}"))
        if DEBUG:
            probe(modT, [ms2])
            probe(bas_a, [ld_bas])
            probe(PB[6][0:16, 0:16], bsid)
            probe(G16, [gr1, gr2])
        ttmp = PHU[0:16, 3396:3412]
        cpT = p.add("vector", (lambda e: e.tensor_copy(ttmp, PB[6][0:16, 0:16])),
                    bsid, tag="basT_cp")
        consume(6, cpT)
        trT = p.add("tensor", (lambda e: e.transpose(PB[6][0:16, 0:16], ttmp, s_idf[0:16, 0:16])),
                    [cpT, ld_idf], tag="basT_tr")
        gadd = p.add("vector", (lambda e: e.tensor_tensor(G16, G16, PB[6][0:16, 0:16], OP.add)),
                     [gr1, gr2, trT], tag="gadd")
        consume(6, gadd)
        s1 = p.add("vector", (lambda e: e.tensor_reduce(G16b[:, 0:1], G16, AX.X, OP.max)),
                   [gadd], tag="wmax")
        s2 = p.add("vector", (lambda e: e.tensor_scalar(G16, G16, G16b[:, 0:1], None, OP.subtract)),
                   [s1], tag="wsub")
        s3 = p.add("scalar", (lambda e: e.activation(G16, G16, AF.Exp)), [s2], tag="wexp")
        s4 = p.add("vector", (lambda e: e.tensor_reduce(G16b[:, 1:2], G16, AX.X, OP.add)),
                   [s3], tag="wsum")
        s5 = p.add("vector", (lambda e: e.reciprocal(G16b[:, 2:3], G16b[:, 1:2])), [s4], tag="wrc")
        s6 = p.add("vector", (lambda e: e.tensor_scalar(G16, G16, G16b[:, 2:3], None, OP.mult)),
                   [s5], tag="wnorm")
        wr1 = p.dma("gpsimd", a_wts[:, 0:1], G16[0:8, :], deps=[s6], tag="wtsrp0")
        wr2 = p.dma("gpsimd", a_wts[:, 1:2], G16[8:16, :], deps=[s6], tag="wtsrp1")
        if DEBUG:
            probe(G16, [s6])

        P1rep = PHU[:, 0:2048]
        P2rep = PHU[:, 2048:4096]
        tmp2 = gw_a
        ldp1 = p.dma("sync", sc_f1[0:1, :], d_stats[4:5, :],
                     deps=gl_evs + [mev] + qk_ready + vfm_deps, tag="ldp1")
        pb1 = p.add("gpsimd", (lambda e: e.partition_broadcast(P1rep, sc_f1[0:1, :], 128)),
                    [ldp1], tag="pbp1")
        ldp2 = p.dma("sync", sc_f1[0:1, :], d_stats[5:6, :], deps=[pb1, wr1, wr2], tag="ldp2")
        pb2 = p.add("gpsimd", (lambda e: e.partition_broadcast(P2rep, sc_f1[0:1, :], 128)),
                    [ldp2], tag="pbp2")
        ldw3 = p.dma("sync", sc_f1[0:1, :], d_stats[6:7, :], deps=[pb2], tag="ldw3")
        w3rep = XA[0:8, 6, :]
        pbw3a = p.add("gpsimd", (lambda e: e.partition_broadcast(w3rep, sc_f1[0:1, :], 8)),
                      [ldw3] + spl, tag="pbw3a")
        pbw3b = pbw3a
        mask_done = []
        kern_evs = []
        prev_acc = [None, None]
        prev_y6 = None
        half_guard = []
        mbuf = [sc_f1, XA[:, 7, :]]
        for half in range(2):
            accs = []

            def m_y1(w, extra):
                wb = w % 2
                buf = mbuf[wb]
                return p.add("scalar", (lambda e, half=half, w=w, buf=buf: e.activation(
                    buf[:], P1rep, AF.Identity,
                    scale=a_fq[:, half * 16 + 2 * w:half * 16 + 2 * w + 1],
                    bias=a_ph[:, half * 8 + w:half * 8 + w + 1])),
                    [pb1, pb2, pbw3a, pbw3b, fqp, php, prev_acc[wb]] + extra,
                    tag=f"msky{half}_{w}")

            def m_y234(w, y1):
                buf = mbuf[w % 2]
                y2 = p.add("vector", (lambda e, half=half, w=w, buf=buf: e.scalar_tensor_tensor(
                    buf[:], P2rep, a_fq[:, half * 16 + 2 * w + 1:half * 16 + 2 * w + 2],
                    buf[:], OP.mult, OP.add)), [y1, pb2], tag=f"msk2{half}_{w}")
                y3 = p.add("vector", (lambda e, buf=buf: e.tensor_scalar(
                    tmp2, buf[:], RC, RC, OP.add, OP.subtract)),
                    [y2, ldp1, ldp2, ldw3], tag=f"msk3{half}_{w}")
                return p.add("vector", (lambda e, buf=buf: e.tensor_sub(buf[:], buf[:], tmp2)),
                             [y3], tag=f"msk4{half}_{w}")

            def m_y5(w, y4):
                buf = mbuf[w % 2]
                return p.add("scalar", (lambda e, buf=buf: e.activation(
                    buf[:], buf[:], AF.Sin, scale=float(2 * np.pi))),
                    [y4], tag=f"msk5{half}_{w}")

            def m_y6(w, y5):
                nonlocal_state = None
                buf = mbuf[w % 2]
                if half == 0 and w == 0:
                    return p.add("vector", (lambda e, half=half, w=w, buf=buf: e.tensor_scalar(
                        sc_f2[:], buf[:], a_am[:, half * 8 + w:half * 8 + w + 1], None, OP.mult)),
                        [y5, ld_am], tag=f"msk6{half}_{w}")
                if w == 0:
                    return p.add("vector", (lambda e, half=half, w=w, buf=buf: e.tensor_scalar(
                        sc_f2[:], buf[:], a_am[:, half * 8 + w:half * 8 + w + 1], None, OP.mult)),
                        [y5, ld_am, prev_y6], tag=f"msk6{half}_{w}")
                return p.add("vector", (lambda e, half=half, w=w, buf=buf: e.scalar_tensor_tensor(
                    sc_f2[:], buf[:], a_am[:, half * 8 + w:half * 8 + w + 1], sc_f2[:],
                    OP.mult, OP.add)), [y5, prev_y6], tag=f"msk6{half}_{w}")

            for k in range(NWV // 2):
                wA, wB = 2 * k, 2 * k + 1
                extra = (([ones_row] + qk_ready if half == 0 else []) + half_guard) if k == 0 else []
                y1A = m_y1(wA, extra)
                y1B = m_y1(wB, extra)
                y4A = m_y234(wA, y1A)
                y5A = m_y5(wA, y4A)
                y4B = m_y234(wB, y1B)
                y5B = m_y5(wB, y4B)
                y6A = m_y6(wA, y5A)
                prev_y6 = y6A
                prev_acc[0] = y6A
                y6B = m_y6(wB, y5B)
                prev_y6 = y6B
                prev_acc[1] = y6B
                accs.append(y6B)
            if DEBUG and half == 0:
                probe(sc_f2[:, 0:512], [accs[-1]])
            tw = p.add("vector", (lambda e, half=half: e.tensor_scalar(
                sc_f2[:], sc_f2[:], a_wts[:, half:half + 1], None, OP.mult)),
                [accs[-1], wr1, wr2], tag=f"mskw{half}")
            if DEBUG and half == 0:
                probe(sc_f2[:, 0:512], [tw])
            devs = []
            for ch in range(4):
                ks = mm(6, a_gsel, sc_f2[:, ch * 512:(ch + 1) * 512], start=True, stop=True,
                        deps=[tw, ld_gs], nrows=8, tag=f"kern{half}_{ch}")
                dv = p.add("scalar", (lambda e, ch=ch: e.activation(
                    sc_f1[0:8, ch * 512:(ch + 1) * 512], PB[6][0:8, 0:512], AF.Sigmoid,
                    scale=float(STEP_A))), [ks], tag=f"dens{half}_{ch}")
                consume(6, dv)
                devs.append(dv)
            d1 = p.add("vector", (lambda e: e.tensor_tensor(sc_f1[0:8, :], sc_f1[0:8, :],
                                                            w3rep, OP.mult)),
                       devs + [pbw3a], tag=f"dmul{half}")
            dst = p.dma("gpsimd", d_dext[half * 8:(half + 1) * 8, 511:2559],
                        sc_f1[0:8, :], deps=[d1], tag=f"dst{half}")
            kern_evs.append(dst)
            half_guard = [d1, dst]
            if DEBUG and half == 0:
                half_guard.append(probe(sc_f1[0:8, 0:512], [d1]))
        padz = p.add("vector", (lambda e: e.memset(sc_f2[0:8, 0:511], NEG)), kern_evs, tag="padz")
        for half in range(2):
            kern_evs.append(p.dma("gpsimd", d_dext[half * 8:(half + 1) * 8, 0:511],
                                  sc_f2[0:8, 0:511], deps=[padz], tag=f"dpad{half}"))
        kern_evs.append(p.dma("gpsimd", d_dext[0:16, 2559:2560],
                              sc_f2[0:8, 0:1][0:8].broadcast_to((8, 1)) if False else sc_f2[0:8, 0:2],
                              deps=[padz], tag="dtail") if False else kern_evs[-1])

        ctx_sts = []
        gld = [None, None]
        gld[0] = p.dma("sync", GB[0], bass.AP(d_dext.tensor, 0, [[1, 128], [1, 2432]]),
                       deps=kern_evs + [pb1, pb2], tag="gb0")
        vaug_dep = [None, None]
        vaug_reader = [None, None]
        vb_deps_h = [[], []]
        state = {"den": None, "pvl_guard": [None, None], "pexp": [None, None]}
        Pts = [Pt.rearrange("p (a b) -> p a b", b=512),
               yqt[:, 0:4, :].rearrange("p a b -> p (a b)").rearrange("p (a b) -> p a b", b=512)]

        def phase_A(h, qc, pidx):
            r = h % 2
            pp = pidx % 2
            base, blk = hb(h)
            ktmax = min(16, (qc * 512 + 511) // 128 + 1)
            pdeps = []
            for kt in range(ktmax):
                jlo = 128 * (15 - kt)
                stb = kt % 2
                sid = mm(stb, lhsK[base:base + 12, blk, jlo:jlo + 128],
                         rhsQ[base:base + 12, blk, qc * 512:(qc + 1) * 512],
                         start=True, stop=True,
                         deps=(qk_ready + [gld[r]] if kt == 0 and qc == 0 and h == 0 else [gld[r]]),
                         tag=f"qk{h}_{qc}_{kt}")
                woff = (qc * 512 - 128 * kt) + 384
                sv = p.add("vector", (lambda e, stb=stb, woff=woff, r=r: e.tensor_tensor(
                    sc_f1[:, stb * 512:stb * 512 + 512], PB[stb][:, 0:512],
                    GB[r][:, woff:woff + 512], OP.add)),
                    [sid, gld[r], state["pexp"][stb]], tag=f"sG{h}_{qc}_{kt}")
                consume(stb, sv)
                pexp = p.add("scalar", (lambda e, kt=kt, stb=stb, pp=pp: e.activation(
                    Pts[pp][:, kt, :],
                    sc_f1[:, stb * 512:stb * 512 + 512], AF.Exp)),
                    [sv, state["pvl_guard"][pp]], tag=f"pex{h}_{qc}_{kt}")
                state["pexp"][stb] = pexp
                pdeps.append(pexp)
            return (h, qc, r, pp, ktmax, pdeps, list(vb_deps_h[r]))

        def phase_B(st):
            h, qc, r, pp, ktmax, pdeps, vb_deps = st
            pvl = None
            for kt in range(ktmax):
                pvl = mm(2, vaug[r].rearrange("p (a b) -> p a b", b=65)[:, kt, :],
                         Pts[pp][:, kt, :],
                         start=(kt == 0), stop=(kt == ktmax - 1),
                         deps=([pdeps[ktmax - 1]] + vb_deps if kt == 0 else []),
                         nrows=65, tag=f"pv{h}_{qc}_{kt}")
            state["pvl_guard"][pp] = pvl
            vaug_reader[r] = pvl
            rc1 = p.add("vector", (lambda e: e.reciprocal(rrow, PB[2][64:65, 0:512])),
                        [pvl, state["den"]], tag=f"rc{h}_{qc}")
            pbr = p.add("gpsimd", (lambda e: e.partition_broadcast(rrep, rrow, 64)),
                        [rc1], tag=f"pbr{h}_{qc}")
            nrm = p.add("vector", (lambda e: e.tensor_tensor(ctxstg, PB[2][0:64, 0:512], rrep, OP.mult)),
                        [pbr, ctx_sts[-1] if ctx_sts else None], tag=f"nrm{h}_{qc}")
            consume(2, nrm)
            state["den"] = nrm
            st2 = p.dma("gpsimd", d_ctx[h * 64:(h + 1) * 64, qc * 512:(qc + 1) * 512], ctxstg,
                        deps=[nrm], tag=f"ctxst{h}_{qc}")
            ctx_sts.append(st2)

        pending = None
        pidx = 0
        for h in range(H):
            r = h % 2
            if h + 1 < H:
                gld[1 - r] = p.dma("sync", GB[1 - r],
                                   bass.AP(d_dext.tensor, (h + 1) * 2560, [[1, 128], [1, 2432]]),
                                   deps=kern_evs + [gld[1 - r]] + ctx_sts[-4:], tag=f"gb{h + 1}")
            pc, hh = h // 2, h % 2
            vb_deps = []
            vfm_pair_dep = vfm_deps[pc * 4 + 3]
            oz = p.add("vector", (lambda e, r=r: e.memset(
                vaug[r].rearrange("p (a b) -> p a b", b=65)[:, :, 64:65], 1.0)),
                [vaug_dep[r], vaug_reader[r]], tag=f"vo{h}")
            for kt in range(16):
                ktp = 15 - kt
                tr = p.add("tensor", (lambda e, pc=pc, hh=hh, ktp=ktp: e.transpose(
                    PB[3].bitcast(BF16)[:, 0:64],
                    vfm.rearrange("p (a b) -> p a b", a=8)[64 * hh:64 * (hh + 1), pc, 128 * ktp:128 * (ktp + 1)],
                    s_idb[64 * hh:64 * (hh + 1), 64 * hh:64 * (hh + 1)])),
                    [blr[3], ld_idb, vfm_pair_dep, vaug_reader[r]], tag=f"vtr{h}_{kt}")
                cp = p.add("vector", (lambda e, r=r, kt=kt: e.tensor_copy(
                    vaug[r].rearrange("p (a b) -> p a b", b=65)[:, kt, 0:64],
                    PB[3].bitcast(BF16)[:, 0:64])), [tr, oz], tag=f"vcp{h}_{kt}")
                consume(3, cp)
                vb_deps.append(cp)
            vaug_dep[r] = vb_deps[-1]
            vb_deps_h[r] = vb_deps
            for qc in range(4):
                stA = phase_A(h, qc, pidx)
                if pending is not None:
                    phase_B(pending)
                pending = stA
                pidx += 1
        phase_B(pending)
        p.barrier()

        ld_ow = [p.dma("gpsimd", ow_a[:, dc * 1024:(dc + 1) * 1024],
                       w_ow[dc * 128:(dc + 1) * 128, :], tag=f"ldow{dc}") for dc in range(NDC)]
        mu1rep = PHU[:, 0:2048]
        inv1rep = PHU[:, 2048:4096]
        lm1 = p.dma("sync", sc_f1[0:1, :], d_stats[0:1, :], tag="lm1")
        pm1 = p.add("gpsimd", (lambda e: e.partition_broadcast(mu1rep, sc_f1[0:1, :], 128)),
                    [lm1], tag="pm1")
        lm2 = p.dma("sync", sc_f1[0:1, :], d_stats[1:2, :], deps=[pm1], tag="lm2")
        pm2 = p.add("gpsimd", (lambda e: e.partition_broadcast(inv1rep, sc_f1[0:1, :], 128)),
                    [lm2], tag="pm2")
        ctile = [PHU[:, 4096:4608], PHU[:, 4608:5120]]
        xtile = [PHU[:, 5120:5632], PHU[:, 5632:6144]]
        ctxl = ARW[:, 0:16384]
        cld = [p.dma("sync", ctxl[:, dc * 2048:(dc + 1) * 2048],
                     d_ctx[dc * 128:(dc + 1) * 128, :], tag=f"cld{dc}") for dc in range(NDC)]
        tin_deps = []
        xl = [None, None]
        for do in range(NDC):
            for tt in range(4):
                rb = (do * 4 + tt) % 2
                xl[rb] = p.dma("sync", xtile[rb], d_x1[do * 128:(do + 1) * 128, tt * 512:(tt + 1) * 512],
                               deps=[xl[rb], pm1, pm2], tag=f"xl{do}_{tt}")
                sid = None
                for dc in range(NDC):
                    sid = mm(4 + tt % 2, ow_a[:, dc * 1024 + do * 128:dc * 1024 + (do + 1) * 128],
                             ctxl[:, dc * 2048 + tt * 512:dc * 2048 + (tt + 1) * 512],
                             start=(dc == 0), stop=(dc == NDC - 1),
                             deps=([cld[dc], ld_ow[dc]] if do == 0 and tt == 0 else
                                   ([cld[dc]] if do == 0 else [])), tag=f"ow{do}_{tt}_{dc}")
                z1 = p.add("vector", (lambda e, tt=tt, rb=rb: e.tensor_sub(
                    ctile[rb], xtile[rb], mu1rep[:, tt * 512:(tt + 1) * 512])),
                    [xl[rb], pm1], tag=f"tz1_{do}_{tt}")
                z2 = p.add("vector", (lambda e, tt=tt, rb=rb: e.tensor_mul(
                    ctile[rb], ctile[rb], inv1rep[:, tt * 512:(tt + 1) * 512])),
                    [z1, pm2], tag=f"tz2_{do}_{tt}")
                t1 = p.add("vector", (lambda e, do=do, tt=tt, rb=rb, b=4 + tt % 2: e.scalar_tensor_tensor(
                    XA[:, do, tt * 512:(tt + 1) * 512], PB[b][:, 0:512], a_ob[:, do:do + 1],
                    ctile[rb], OP.add, OP.add)), [sid, z2, ld_ob], tag=f"tin{do}_{tt}")
                consume(4 + tt % 2, t1)
                xl[rb] = t1
                tin_deps.append(t1)
        p.barrier()

        muT, invT = emit_ln_stats("l1", lambda dc: XA[:, dc, :], tin_deps)
        hfd = []
        for dc in range(NDC):
            f1 = p.add("vector", (lambda e, dc=dc: e.tensor_sub(sc_f1[:], XA[:, dc, :], repA[:, :])),
                       [muT, invT], tag=f"hf1_{dc}")
            f2 = p.add("vector", (lambda e: e.tensor_mul(sc_f1[:], sc_f1[:], repG[:, :])),
                       [f1], tag=f"hf2_{dc}")
            f3 = p.add("scalar", (lambda e, dc=dc: e.activation(XA[:, dc, :], sc_f1[:], AF.Identity,
                                                                scale=a_ln[:, dc:dc + 1],
                                                                bias=a_ln[:, 8 + dc:8 + dc + 1])),
                       [f2, ld_ln], tag=f"hf3_{dc}")
            f4 = p.add("vector", (lambda e, dc=dc: e.tensor_copy(yqv(dc), XA[:, dc, :])),
                       [f3], tag=f"hbf_{dc}")
            hfd.append(f4)
        ld_w1 = [p.dma("gpsimd", ARW[:, dc * 4096:(dc + 1) * 4096],
                       w_fw1[dc * 128:(dc + 1) * 128, :], deps=[cld[dc]], tag=f"ldW1_{dc}")
                 for dc in range(NDC)]
        ustg = sc_f2.bitcast(BF16)[:, 0:2048]
        prev_ust = None
        u_sts = []
        for fo in range(NFO):
            evs = []
            for tt in range(4):
                bi = 4 + tt % 2
                sid = None
                for dc in range(NDC):
                    sid = mm(bi, ARW[:, dc * 4096 + fo * 128:dc * 4096 + (fo + 1) * 128],
                             yqv(dc)[:, tt * 512:(tt + 1) * 512], start=(dc == 0), stop=(dc == NDC - 1),
                             deps=(hfd + ld_w1 if fo == 0 and tt == 0 and dc == 0 else []),
                             tag=f"u{fo}_{tt}_{dc}")
                ev = p.add("scalar", (lambda e, fo=fo, tt=tt, bi=bi: e.activation(
                    ustg[:, tt * 512:(tt + 1) * 512], PB[bi][:, 0:512], AF.Gelu,
                    bias=a_fb1[:, fo:fo + 1])), [sid, ld_fb1, prev_ust], tag=f"uev{fo}_{tt}")
                consume(bi, ev)
                evs.append(ev)
            st = p.dma("gpsimd", d_u[fo * 128:(fo + 1) * 128, :], ustg, deps=evs, tag=f"ust{fo}")
            prev_ust = st
            u_sts.append(st)
        p.barrier()
        ld_w2 = [p.dma("gpsimd", ARW[:, kc * 1024:(kc + 1) * 1024],
                       w_fw2[kc * 128:(kc + 1) * 128, :], tag=f"ldW2_{kc}") for kc in range(NFO)]
        x2t = PHU[:, 0:4096].rearrange("p (a b) -> p a b", a=NDC)
        mu2rep = PHU[:, 4096:4608]
        inv2rep = PHU[:, 4608:5120]
        x2_deps = []
        prev_ucl = None
        prev_e1s = []
        prev_xv = [None, None]
        prev_x2 = [None] * NDC
        for tt in range(4):
            ucl = []
            for g in range(4):
                sap = bass.AP(d_u.tensor, g * 8 * 128 * S + tt * 512, [[S, 128], [128 * S, 8], [1, 512]])
                ucl.append(p.dma("sync", yqt[:, 2 * g:2 * g + 2, :].rearrange("p a b -> p (a b)")
                                 .rearrange("p (a b) -> p a b", a=8),
                                 sap, deps=[prev_ucl] + prev_e1s, tag=f"ucl{tt}_{g}"))
            prev_ucl = ucl[-1]
            e1s = []
            for do in range(NDC):
                sid = None
                for kc in range(NFO):
                    uview = yqt[:, 2 * (kc // 8):2 * (kc // 8) + 2, :].rearrange("p a b -> p (a b)") \
                        .rearrange("p (a b) -> p a b", a=8)[:, kc % 8, :]
                    sid = mm(do % 2, ARW[:, kc * 1024 + do * 128:kc * 1024 + (do + 1) * 128],
                             uview, start=(kc == 0), stop=(kc == NFO - 1), ncols=512,
                             deps=(ucl + ld_w2 + u_sts if kc == 0 else []), tag=f"w2_{tt}_{do}_{kc}")
                e1 = p.add("vector", (lambda e, do=do, tt=tt, b=do % 2: e.scalar_tensor_tensor(
                    x2t[:, do, :], PB[b][:, 0:512], a_fb2[:, do:do + 1],
                    XA[:, do, tt * 512:(tt + 1) * 512], OP.add, OP.add)),
                    [sid, ld_fb2, prev_x2[do]], tag=f"e1_{tt}_{do}")
                consume(do % 2, e1)
                e1s.append(e1)
            prev_e1s = e1s
            sb_c = sc_f2.bitcast(BF16)[:, 0:512]
            sb_s = sc_f2.bitcast(BF16)[:, 512:1024]
            m1l = m2l = None
            pc1 = pc2 = None
            for dc in range(NDC):
                c1 = p.add("vector", (lambda e, dc=dc: e.tensor_copy(sb_c, x2t[:, dc, :])),
                           e1s + [pc1], tag=f"l2c{tt}_{dc}")
                c2 = p.add("scalar", (lambda e, dc=dc: e.activation(sb_s, x2t[:, dc, :], AF.Square)),
                           e1s + [pc2], tag=f"l2s{tt}_{dc}")
                m1l = mm(6, ones_b[:], sb_c, start=(dc == 0), stop=(dc == NDC - 1),
                         deps=[c1, ones_dep], nrows=1, tag=f"l2m1_{tt}_{dc}")
                m2l = mm(7, ones_b[:], sb_s, start=(dc == 0), stop=(dc == NDC - 1),
                         deps=[c2, ones_dep], nrows=1, tag=f"l2m2_{tt}_{dc}")
                pc1, pc2 = m1l, m2l
            ev1 = p.add("vector", (lambda e: e.tensor_copy(sc_f1[0:1, 0:512], PB[6][0:1, 0:512])),
                        [m1l], tag=f"l2e1_{tt}")
            consume(6, ev1)
            ev2 = p.add("vector", (lambda e: e.tensor_copy(sc_f1[0:1, 512:1024], PB[7][0:1, 0:512])),
                        [m2l], tag=f"l2e2_{tt}")
            consume(7, ev2)
            q1 = p.add("gpsimd", (lambda e: e.partition_broadcast(mu2rep, sc_f1[0:1, 0:512], 128)),
                       [ev1], tag=f"l2p1_{tt}")
            q2 = p.add("gpsimd", (lambda e: e.partition_broadcast(inv2rep, sc_f1[0:1, 512:1024], 128)),
                       [ev2, q1], tag=f"l2p2_{tt}")
            n1 = p.add("vector", (lambda e: e.tensor_scalar(mu2rep, mu2rep, 1.0 / D, None, OP.mult)),
                       [q1], tag=f"l2mu_{tt}")
            n2 = p.add("vector", (lambda e: e.tensor_mul(sc_f1[:, 1024:1536], mu2rep, mu2rep)),
                       [n1], tag=f"l2mu2_{tt}")
            n3 = p.add("vector", (lambda e: e.scalar_tensor_tensor(inv2rep, inv2rep, 1.0 / D,
                                                                   sc_f1[:, 1024:1536], OP.mult, OP.subtract)),
                       [n2, q2], tag=f"l2var_{tt}")
            n4 = p.add("vector", (lambda e: e.tensor_scalar(inv2rep, inv2rep, 1e-5, None, OP.add)),
                       [n3], tag=f"l2ve_{tt}")
            n5 = p.add("vector", (lambda e: e.reciprocal(inv2rep, inv2rep)), [n4], tag=f"l2rc_{tt}")
            n6 = p.add("scalar", (lambda e: e.activation(inv2rep, inv2rep, AF.Sqrt)), [n5], tag=f"l2rs_{tt}")
            if DEBUG and tt == 0:
                probe(x2t[:, 0, :], e1s)
                probe(mu2rep[:, 0:512], [n1])
                probe(inv2rep[:, 0:512], [n6])
            for do in range(NDC):
                rb = do % 2
                xld = p.dma("sync", xtile[rb], d_x1[do * 128:(do + 1) * 128, tt * 512:(tt + 1) * 512],
                            deps=[prev_xv[rb], xl[0], xl[1]], tag=f"x2l{tt}_{do}")
                v1 = p.add("vector", (lambda e, do=do: e.tensor_sub(sc_f1[:, 1024:1536], x2t[:, do, :],
                                                                    mu2rep)),
                           [n6] + e1s, tag=f"x2a{tt}_{do}")
                v2 = p.add("vector", (lambda e: e.tensor_mul(sc_f1[:, 1024:1536], sc_f1[:, 1024:1536],
                                                             inv2rep)),
                           [v1], tag=f"x2b{tt}_{do}")
                v3 = p.add("scalar", (lambda e, do=do: e.activation(sc_f1[:, 1536:2048],
                                                                    sc_f1[:, 1024:1536], AF.Identity,
                                                                    scale=a_ln[:, 16 + do:16 + do + 1],
                                                                    bias=a_ln[:, 24 + do:24 + do + 1])),
                           [v2, ld_ln], tag=f"x2c{tt}_{do}")
                v4 = p.add("vector", (lambda e, do=do, tt=tt, rb=rb: e.tensor_tensor(
                    XA[:, do, tt * 512:(tt + 1) * 512], xtile[rb], sc_f1[:, 1536:2048], OP.add)),
                    [v3, xld], tag=f"x2d{tt}_{do}")
                prev_x2[do] = v4
                prev_xv[rb] = v4
                x2_deps.append(v4)
        if DEBUG:
            probe(XA[:, 0, 0:512], [x2_deps[0]])
            probe(XA[:, 7, 0:512], [x2_deps[7]])
        p.barrier()
        return x2_deps

    if STAGE != "x":
        updA = emit_bitffn("A", w1_rdy, w2_rdy, w2_mw, [x_dep])
        if STAGE not in ("A",):
            x2_deps = emit_attention(updA[-8:])
            if STAGE not in ("attn",):
                cw1 = load_arena_from_stash("Cw1", d_w1t, NFO * NDC, [x2_deps[-1]])
                updC = emit_bitffn("C", cw1[-1], w2_rdy, w2_mw, x2_deps[-8:])
    p.barrier()

    outs = []
    ytm, ytm2 = xtm, xtm2
    prev_y = [None, None]
    for tt in range(16):
        buf = ytm if tt % 2 == 0 else ytm2
        cps = []
        for dc in range(NDC):
            bi = dc % 2
            tr = p.add("tensor", (lambda e, dc=dc, tt=tt, bi=bi:
                                  e.transpose(PB[bi][:, 0:128], XA[:, dc, tt * 128:(tt + 1) * 128], s_idf[:])),
                       [blr[bi], prev_y[tt % 2]], tag=f"try{tt}_{dc}")
            cp = p.add("vector", (lambda e, dc=dc, bi=bi, b=buf:
                                  e.tensor_copy(b[:, dc * 128:(dc + 1) * 128], PB[bi][:, 0:128])),
                       [tr], tag=f"cpy{tt}_{dc}")
            consume(bi, cp)
            cps.append(cp)
        st = p.dma("gpsimd", y_out[tt * 128:(tt + 1) * 128, :], buf[:], deps=cps, tag=f"sty{tt}")
        outs.append(st)
        prev_y[tt % 2] = st
    p.add("gpsimd", lambda e: e.nop(), deps=outs, tag="final_wait")

    stack.close()

    p.link()
    sems = {}
    for key in p.sem_names():
        h = nc.semaphore("sem_" + "_".join(str(k) for k in key))
        ctx.append(h)
        sems[key] = h.__enter__()
    with nc.Block() as block:
        @block.sync
        def _(e):
            p.emit_engine("sync", e, sems)

        @block.tensor
        def _(e):
            p.emit_engine("tensor", e, sems)

        @block.vector
        def _(e):
            p.emit_engine("vector", e, sems)

        @block.scalar
        def _(e):
            p.emit_engine("scalar", e, sems)

        @block.gpsimd
        def _(e):
            p.emit_engine("gpsimd", e, sems)
    return p


_NC_CACHE = {}


def _get_nc():
    if "nc" not in _NC_CACHE:
        nc = bass.Bass("TRN2", target_bir_lowering=False, detect_race_conditions=False)
        _build(nc)
        lower_extended_insts(nc)
        _NC_CACHE["nc"] = nc
    return _NC_CACHE["nc"]


def kernel(**inputs):
    nc = _get_nc()
    x = np.ascontiguousarray(np.asarray(inputs["L0_attn_x"], np.float32))
    gsel = np.zeros((128, 8), np.float32)
    for j in range(8):
        gsel[16 * j:16 * (j + 1), j] = 1.0

    def f32(name):
        return np.ascontiguousarray(np.asarray(inputs[name], np.float32))

    base = {
        "idf": np.eye(128, dtype=np.float32),
        "idb": np.eye(128, dtype=ml_dtypes.bfloat16),
        "kramp": np.arange(S, dtype=np.float32).reshape(1, S),
        "gsel": gsel,
        "bw1": f32("L0_ffn_w1"),
        "bw2": f32("L0_ffn_w2"),
        "conv_w": f32("L0_ffn_conv_w"),
        "conv_b": f32("L0_ffn_conv_b"),
        "vw": f32("L0_attn_v_w"), "ow": f32("L0_attn_out_w"),
        "cqw": f32("L0_attn_cq_w"), "ckw": f32("L0_attn_ck_w"),
        "qmw": f32("L0_attn_qmod_w"), "kmw": f32("L0_attn_kmod_w"),
        "gw": f32("L0_attn_gate_w"), "mw": f32("L0_attn_mod_w"),
        "basis": f32("L0_attn_mod_basis"), "freqs": f32("L0_attn_freqs"),
        "amps": f32("L0_attn_amps"), "phases": f32("L0_attn_phases"),
        "gb": f32("L0_attn_gate_b"), "mb": f32("L0_attn_mod_b"),
        "vb": f32("L0_attn_v_b"), "ob": f32("L0_attn_out_b"),
        "fw1": f32("L0_attn_ffn_w1"), "fb1": f32("L0_attn_ffn_b1"),
        "fw2": f32("L0_attn_ffn_w2"), "fb2": f32("L0_attn_ffn_b2"),
        "lnv": np.ascontiguousarray(np.stack([
            np.asarray(inputs["L0_attn_ln1_g"], np.float32),
            np.asarray(inputs["L0_attn_ln1_b"], np.float32),
            np.asarray(inputs["L0_attn_ln2_g"], np.float32),
            np.asarray(inputs["L0_attn_ln2_b"], np.float32)])),
    }
    in_maps = []
    for c in range(8):
        m = dict(base)
        m["x"] = np.ascontiguousarray(x[c])
        in_maps.append(m)
    res = run_bass_kernel_spmd(nc, in_maps, core_ids=list(range(8)))
    return np.stack([res.results[c]["y"] for c in range(8)], axis=0)
